# version 12
# baseline (speedup 1.0000x reference)
import math
from contextlib import ExitStack

import numpy as np
import ml_dtypes

import concourse.bass as bass
import concourse.mybir as mybir
from concourse.bass_utils import run_bass_kernel_spmd

F32 = mybir.dt.float32
BF16 = mybir.dt.bfloat16
I8 = mybir.dt.int8
I32 = mybir.dt.int32
AF = mybir.ActivationFunctionType
ALU = mybir.AluOpType
AX = mybir.AxisListType

BF = ml_dtypes.bfloat16


ENGINES = ("tensor", "vector", "scalar", "gpsimd", "sync")
DMA_QUEUES = {"qs": "sync", "qa": "scalar", "qg": "gpsimd"}
STREAMS = ENGINES + tuple(DMA_QUEUES)


class _Op:
    __slots__ = ("stream", "fn", "reads", "writes", "idx", "dep", "waits", "inc")

    def __init__(self, stream, fn, reads, writes):
        self.stream = stream
        self.fn = fn
        self.reads = tuple(reads)
        self.writes = tuple(writes)
        self.dep = None
        self.waits = None
        self.inc = False


RELAXED_STREAMS = {"tensor"}


class Prog:
    def __init__(self, nc):
        self.nc = nc
        self.ops = []
        self.stream_ops = {s: [] for s in STREAMS}
        self._consts = {}
        self._const_stack = ExitStack()

    def constcol(self, value):
        if value not in self._consts:
            t = self.nc.alloc_sbuf_tensor(f"cst_{len(self._consts)}", [128, 1], F32, side="right")
            k = ("cst", value)
            self.op("vector", lambda e: e.memset(t[:], float(value)), writes=[k])
            self._consts[value] = (t, k)
        return self._consts[value]

    def op(self, stream, fn, reads=(), writes=()):
        assert stream in STREAMS, stream
        o = _Op(stream, fn, reads, writes)
        o.idx = len(self.stream_ops[stream])
        self.stream_ops[stream].append(o)
        self.ops.append(o)
        return o

    def dma(self, queue, out, in_, reads=(), writes=(), **kw):
        def fn(e):
            return e.dma_start(out=out, in_=in_, **kw)
        return self.op(queue, fn, reads, writes)

    def barrier(self):
        lasts = {s: len(self.stream_ops[s]) - 1 for s in STREAMS
                 if self.stream_ops[s]}
        for eng in ENGINES:
            dep = {s: i for s, i in lasts.items() if s != eng}
            o = self.op(eng, lambda e: e.nop())
            o.dep = dep

    def resolve(self):
        last_w = {}
        readers = {}
        for o in self.ops:
            dep = dict(o.dep) if o.dep is not None else {}

            def add(s, i):
                if s == o.stream and s in RELAXED_STREAMS:
                    return
                if dep.get(s, -1) < i:
                    dep[s] = i
            for k in o.reads:
                if k in last_w:
                    add(*last_w[k])
            for k in o.writes:
                if k in last_w:
                    add(*last_w[k])
                for (s, i) in readers.get(k, ()):
                    add(s, i)
            o.dep = dep
            for k in o.reads:
                readers.setdefault(k, []).append((o.stream, o.idx))
            for k in o.writes:
                last_w[k] = (o.stream, o.idx)
                readers[k] = []
        cur = {s: {} for s in STREAMS}
        snap = {s: [] for s in STREAMS}
        demands = {s: set() for s in STREAMS}
        for o in self.ops:
            g = cur[o.stream]
            waits = []
            for t in sorted(o.dep, key=lambda t: o.dep[t]):
                i = o.dep[t]
                if g.get(t, -1) >= i:
                    continue
                waits.append((t, i))
                demands[t].add(i)
                sn = snap[t][i]
                for t2, i2 in sn.items():
                    if g.get(t2, -1) < i2:
                        g[t2] = i2
                if g.get(t, -1) < i:
                    g[t] = i
            o.waits = waits
            sng = dict(g)
            sng[o.stream] = o.idx
            snap[o.stream].append(sng)
        self.wait_val = {}
        for s in STREAMS:
            step = 16 if s in DMA_QUEUES else 1
            cnt = 0
            for o in self.stream_ops[s]:
                if s in DMA_QUEUES or o.idx in demands[s]:
                    o.inc = True
                    cnt += 1
                self.wait_val[(s, o.idx)] = cnt * step

    def emit(self, block):
        nc = self.nc
        self.sems = {}
        self._sem_stack = ExitStack()
        for s in STREAMS:
            if self.stream_ops[s]:
                self.sems[s] = self._sem_stack.enter_context(
                    nc.semaphore(f"sem_{s}"))

        per_engine = {e: [] for e in ENGINES}
        for o in self.ops:
            eng = DMA_QUEUES.get(o.stream, o.stream)
            per_engine[eng].append(o)

        def runner(items):
            def run(e):
                for o in items:
                    for (t, i) in o.waits:
                        e.wait_ge(self.sems[t], self.wait_val[(t, i)])
                    inst = o.fn(e)
                    if o.inc:
                        step = 16 if o.stream in DMA_QUEUES else 1
                        inst.then_inc(self.sems[o.stream], step)
            return run
        for eng in ENGINES:
            if per_engine[eng]:
                getattr(block, eng)(runner(per_engine[eng]))



NT = 4096
G = 2560
OWN = 2048
OWN_OFF = 256
S = 8192
EPS = 1e-5


def layernorm_tile(P, ctx, nc, name, sx_ap, xn_out_ap, rkeys, wkeys, bufs):
    st48, st2, sd, rstd, nb = bufs
    kst = (name, "st")
    eps_ap, eps_k = P.constcol(EPS)
    for j in range(8):
        P.op("vector", lambda e, j=j: e.bn_stats(st48[:, j * 6:(j + 1) * 6],
                                                 sx_ap[:, j * 512:(j + 1) * 512]),
             reads=rkeys, writes=[kst])
    P.op("vector", lambda e: e.bn_aggr(st2[:], st48[:]), reads=[kst], writes=[kst])
    P.op("scalar", lambda e: e.activation(out=sd[:], in_=st2[:, 1:2], func=AF.Sqrt,
                                          bias=eps_ap[:, 0:1]),
         reads=[kst, eps_k], writes=[kst])
    P.op("vector", lambda e: e.reciprocal(rstd[:], sd[:]), reads=[kst], writes=[kst])
    P.op("vector", lambda e: e.tensor_scalar(nb[:], st2[:, 0:1], rstd[:, 0:1], -1.0,
                                             ALU.mult, ALU.mult),
         reads=[kst], writes=[kst])
    P.op("scalar", lambda e: e.activation(out=xn_out_ap, in_=sx_ap, func=AF.Identity,
                                          bias=nb[:, 0:1], scale=rstd[:, 0:1]),
         reads=list(rkeys) + [kst], writes=wkeys)



P0_EXP_BIAS = -10.0


def build_p0(nc, P, ctx, io):
    E = ctx.enter_context
    xbat, xreg, wqkv, bqkv = io["xbat"], io["xreg"], io["wqkv"], io["bqkv"]
    wo0, bo0, x1 = io["wo0"], io["bo0"], io["x1"]
    xnh, xnregh, kTh, vTh, qTh, oTh = (io["xn"], io["xnreg"], io["kT"],
                                       io["vT"], io["qT"], io["oT"])

    bqc = E(nc.sbuf_tensor("p0_bqc", [128, 24], F32))
    bq32 = E(nc.sbuf_tensor("p0_bq32", [128, 8], F32))
    ones_r = E(nc.sbuf_tensor("p0_ones_r", [1, 128], BF16))
    ones_rf = E(nc.sbuf_tensor("p0_ones_rf", [1, 128], F32))
    ones_c = E(nc.sbuf_tensor("p0_ones_c", [128, 1], BF16))
    st48 = E(nc.sbuf_tensor("p0_st48", [128, 48], F32))
    st2 = E(nc.sbuf_tensor("p0_st2", [128, 2], F32))
    sd = E(nc.sbuf_tensor("p0_sd", [128, 1], F32))
    rstd = E(nc.sbuf_tensor("p0_rstd", [128, 1], F32))
    nbv = E(nc.sbuf_tensor("p0_nb", [128, 1], F32))
    ln_bufs = (st48, st2, sd, rstd, nbv)

    bqkv2 = bqkv.rearrange("(a p) -> a p", p=128)
    for a in range(24):
        P.dma("qs", bqc[:, a:a + 1],
              bqkv2[a:a + 1, :].rearrange("a p -> p a"), writes=[("p0bq",)],
              allow_slow_non_contiguous=True)
    P.op("vector", lambda e: e.tensor_scalar(bq32[:], bqc[:, 0:8], 1.0 / 32.0, None,
                                             ALU.mult),
         reads=[("p0bq",)], writes=[("p0bq32",)])
    P.op("vector", lambda e: e.memset(ones_r[:], 1.0), writes=[("p0ones",)])
    P.op("vector", lambda e: e.memset(ones_rf[:], 1.0), writes=[("p0onesf",)])
    P.op("vector", lambda e: e.memset(ones_c[:], 1.0), writes=[("p0onesc",)])
    ebias_ap, ebias_k = P.constcol(P0_EXP_BIAS)

    with ExitStack() as sA:
        EA = sA.enter_context
        arena = [EA(nc.sbuf_tensor(f"p0A_ar{i}", [128, 32 * 1024], BF16))
                 for i in range(2)]
        sx = [EA(nc.sbuf_tensor(f"p0A_sx{i}", [128, 4096], F32)) for i in range(2)]
        sxn = [EA(nc.sbuf_tensor(f"p0A_sxn{i}", [128, 4096], BF16)) for i in range(2)]
        wkv = [EA(nc.sbuf_tensor(f"p0A_wkv{i}", [128, 32 * 128], BF16)) for i in range(2)]
        kvs = [EA(nc.sbuf_tensor(f"p0A_kvs{i}", [128, 512], BF16)) for i in range(8)]
        pkv = [EA(nc.psum_tensor(f"p0A_pkv{i}", [128, 512], F32)) for i in range(8)]

        secs = []
        rdone = 0
        for s in range(8):
            lst = []
            for t in range(8):
                row0 = s * 1024 + t * 128
                lst.append((xbat[row0:row0 + 128, :], xnh[row0:row0 + 128, :],
                            ("p0A.xnh", s), "qa"))
            nreg = (20 * (s + 1)) // 8 - (20 * s) // 8
            for _ in range(nreg):
                rt = rdone
                rdone += 1
                lst.append((xreg[rt * 128:(rt + 1) * 128, :],
                            xnregh[rt * 128:(rt + 1) * 128, :],
                            ("p0R.xnh",), "qa"))
            secs.append(lst)

        ctr = [0]

        def emit_ln_section(s):
            for (src, dst, key, q) in secs[s]:
                b = ctr[0] % 2
                ctr[0] += 1
                kx = ("p0A.x", b)
                P.dma("qs", sx[b][:], src, writes=[kx])
                kxn = ("p0A.xn", b)
                layernorm_tile(P, ctx, nc, "p0ln", sx[b][:], sxn[b][:], [kx],
                               [kxn], ln_bufs)
                P.dma(q, dst, sxn[b][:], reads=[kxn], writes=[key])

        emit_ln_section(0)
        for ch in range(8):
            ab = ch % 2
            for c in range(32):
                P.dma("qs", arena[ab][:, c * 1024:(c + 1) * 1024],
                      xnh[ch * 1024:ch * 1024 + 1024, c * 128:(c + 1) * 128],
                      transpose=True,
                      reads=[("p0A.xnh", ch)], writes=[("p0A.ar", ab)])
            if ch + 1 < 8:
                emit_ln_section(ch + 1)
            for m in range(16):
                wb = m % 2
                kw = ("p0A.w", wb)
                P.dma("qg", wkv[wb][:],
                      wqkv[8 + m:9 + m].rearrange("o p a f -> p (o a f)"),
                      writes=[kw])
                dsth = kTh if m < 8 else vTh
                dr0 = (m % 8) * 128
                for tc in range(2):
                    pb = (m % 4) * 2 + tc
                    kp = ("p0A.p", pb)
                    for c in range(32):
                        P.op("tensor", lambda e, c=c, tc=tc, wb=wb, pb=pb, ab=ab,
                             arena=arena, wkv=wkv, pkv=pkv: e.matmul(
                            pkv[pb][:], wkv[wb][:, c * 128:(c + 1) * 128],
                            arena[ab][:, c * 1024 + tc * 512:
                                       c * 1024 + tc * 512 + 512],
                            start=(c == 0), stop=(c == 31)),
                            reads=[kw, ("p0A.ar", ab)], writes=[kp])
                    ks = ("p0A.s", pb)
                    P.op("scalar", lambda e, m=m, pb=pb, kvs=kvs, pkv=pkv: e.activation(
                        out=kvs[pb][:], in_=pkv[pb][:], func=AF.Identity,
                        bias=bqc[:, 8 + m:9 + m]),
                        reads=[kp, ("p0bq",)], writes=[ks, kp])
                    P.dma("qa", dsth[dr0:dr0 + 128,
                                     ch * 1024 + tc * 512: ch * 1024 + tc * 512 + 512],
                          kvs[pb][:], reads=[ks], writes=[("p0A.kvh",)])
    P.barrier()

    with ExitStack() as sQ:
        EQ = sQ.enter_context
        arena = EQ(nc.sbuf_tensor("p0Q_arena", [128, 32 * 1280], BF16))
        wq = [EQ(nc.sbuf_tensor(f"p0Q_wq{i}", [128, 32 * 128], BF16)) for i in range(2)]
        qst = [EQ(nc.sbuf_tensor(f"p0Q_qst{i}", [128, 512], BF16)) for i in range(2)]
        pq = [EQ(nc.psum_tensor(f"p0Q_pq{i}", [128, 512], F32)) for i in range(2)]
        for ps in range(2):
            r0 = ps * 1280
            for c in range(32):
                P.dma("qs", arena[:, c * 1280:c * 1280 + 1280],
                      xnregh[r0:r0 + 1280, c * 128:(c + 1) * 128], transpose=True,
                      reads=[("p0R.xnh",)], writes=[("p0Q.ar",)])
            for m in range(8):
                wb = m % 2
                kw = ("p0Q.w", wb)
                P.dma("qg", wq[wb][:],
                      wqkv[m:m + 1].rearrange("o p a f -> p (o a f)"),
                      writes=[kw])
                for tc, tn in ((0, 512), (1, 512), (2, 256)):
                    pb = tc % 2
                    kp = ("p0Q.p", pb)
                    for c in range(32):
                        P.op("tensor", lambda e, c=c, tc=tc, wb=wb, pb=pb, tn=tn,
                             arena=arena, wq=wq, pq=pq: e.matmul(
                            pq[pb][:, 0:tn], wq[wb][:, c * 128:(c + 1) * 128],
                            arena[:, c * 1280 + tc * 512: c * 1280 + tc * 512 + tn],
                            start=(c == 0), stop=(c == 31)),
                            reads=[kw, ("p0Q.ar",)], writes=[kp])
                    ks = ("p0Q.s", pb)
                    P.op("scalar", lambda e, m=m, pb=pb, tn=tn, qst=qst, pq=pq: e.activation(
                        out=qst[pb][:, 0:tn], in_=pq[pb][:, 0:tn], func=AF.Identity,
                        bias=bq32[:, m:m + 1], scale=1.0 / 32.0),
                        reads=[kp, ("p0bq32",)], writes=[ks, kp])
                    P.dma("qs", qTh[m * 128:(m + 1) * 128,
                                    r0 + tc * 512: r0 + tc * 512 + tn],
                          qst[pb][:, 0:tn], reads=[ks], writes=[("p0Q.h",)])
    P.barrier()

    with ExitStack() as sB:
        EB = sB.enter_context
        PT = EB(nc.sbuf_tensor("p0B_PT", [128, 64 * 512], BF16))
        qTc = [EB(nc.sbuf_tensor(f"p0B_qT{i}", [128, 8 * 512], BF16)) for i in range(2)]
        kTt = [EB(nc.sbuf_tensor(f"p0B_kT{i}", [128, 8 * 512], BF16)) for i in range(3)]
        vb = [EB(nc.sbuf_tensor(f"p0B_v{i}", [128, 512], BF16)) for i in range(4)]
        oTc = EB(nc.sbuf_tensor("p0B_oT", [128, 8 * 512], BF16))
        lrec = EB(nc.sbuf_tensor("p0B_lrec", [1, 512], F32))
        lsb = EB(nc.sbuf_tensor("p0B_lsb", [1, 512], F32))
        lrecb = EB(nc.sbuf_tensor("p0B_lrecb", [128, 512], F32))
        pst = [EB(nc.psum_tensor(f"p0B_st{i}", [128, 512], F32)) for i in range(2)]
        pl = EB(nc.psum_tensor("p0B_l", [128, 512], F32))
        pacc = [EB(nc.psum_tensor(f"p0B_acc{i}", [128, 512], F32)) for i in range(4)]
        pbc = EB(nc.psum_tensor("p0B_bc", [128, 512], F32))
        oTg = [EB(nc.sbuf_tensor(f"p0C_oT{i}", [128, 8 * 1024], BF16)) for i in range(2)]
        bo_rowb = EB(nc.sbuf_tensor("p0_bo_rowb", [1, 4096], BF16))
        wo = [EB(nc.sbuf_tensor(f"p0C_wo{d}", [128, 2048], BF16)) for d in range(8)]
        xr = [EB(nc.sbuf_tensor(f"p0C_xr{i}", [128, 512], F32)) for i in range(2)]
        x1s = [EB(nc.sbuf_tensor(f"p0C_x1s{i}", [128, 512], F32)) for i in range(2)]
        pc = [pst[0], pst[1], pacc[0], pacc[1], pacc[2], pacc[3], pbc, pl]
        bkey = [("p0B.st", 0), ("p0B.st", 1), ("p0B.acc", 0), ("p0B.acc", 1),
                ("p0B.acc", 2), ("p0B.acc", 3), ("p0B.bc",), ("p0B.l",)]

        P.dma("qg", bo_rowb[:], bo0.rearrange("(a f) -> a f", a=1), writes=[("p0bob",)])

        grps = [(0, 8, (0, 1)), (1024, 8, (2, 3)), (2048, 4, (4,))]

        def emit_oTg_load(gi):
            g0, ntile, qcs = grps[gi]
            gb = gi % 2
            P.dma("qs", oTg[gb][:, 0:8 * 128 * ntile].rearrange(
                "p (m k) -> p m k", m=8),
                  oTh[:, g0:g0 + ntile * 128].rearrange("(m p) k -> p m k", p=128),
                  reads=[("p0B.oTh", q) for q in qcs], writes=[("p0C.oT", gb)])

        def emit_wo_loads(och):
            for d in range(8):
                P.dma("qg", wo[d][:],
                      wo0[d * 128:(d + 1) * 128, och * 2048:(och + 1) * 2048],
                      writes=[("p0C.w", d)])

        for qc in range(5):
            qb = qc % 2
            kqt = ("p0B.qT", qb)
            P.dma("qs", qTc[qb][:],
                  qTh[:, qc * 512:(qc + 1) * 512].rearrange("(m p) k -> p m k", p=128),
                  reads=[("p0Q.h",)], writes=[kqt])

            def lacc(idx, half):
                kt, ks_ = divmod(idx, 4)
                kPT = ("p0B.PT", kt, ks_)
                vbb = idx % 4
                kv = ("p0B.v", vbb)
                if half == 0:
                    P.op("tensor", lambda e, idx=idx: e.matmul(
                        pl[0:1, :], ones_c[:], PT[:, idx * 512:(idx + 1) * 512],
                        start=(idx == 0), stop=(idx == 63)),
                        reads=[kPT, ("p0onesc",)], writes=[("p0B.l",)])
                for d in range(4):
                    P.op("tensor", lambda e, d=d, idx=idx, vbb=vbb, half=half: e.matmul(
                        pacc[d][:], vb[vbb][:, d * 128:(d + 1) * 128],
                        PT[:, idx * 512:(idx + 1) * 512],
                        start=(idx == 0), stop=(idx == 63)),
                        reads=[kv, kPT], writes=[("p0B.acc", d)])

            for half in range(2):
                for idx in range(64):
                    kt, ks_ = divmod(idx, 4)
                    vbb = idx % 4
                    kv = ("p0B.v", vbb)
                    P.dma("qs", vb[vbb][:],
                          vTh[half * 512:half * 512 + 512, idx * 128:idx * 128 + 128],
                          transpose=True, reads=[("p0A.kvh",)], writes=[kv])
                    if half == 0:
                        kb = kt % 3
                        kk = ("p0B.kT", kb)
                        if ks_ == 0:
                            P.dma("qs" if kt % 2 == 0 else "qg", kTt[kb][:],
                                  kTh[:, kt * 512:(kt + 1) * 512].rearrange(
                                      "(m p) k -> p m k", p=128),
                                  reads=[("p0A.kvh",)], writes=[kk])
                        sb = idx % 2
                        kst_ = ("p0B.st", sb)
                        for d in range(8):
                            P.op("tensor", lambda e, d=d, ks_=ks_, sb=sb, kb=kb,
                                 qb=qb: e.matmul(
                                pst[sb][:],
                                kTt[kb][:, d * 512 + ks_ * 128: d * 512 + ks_ * 128 + 128],
                                qTc[qb][:, d * 512:(d + 1) * 512],
                                start=(d == 0), stop=(d == 7)),
                                reads=[kk, kqt], writes=[kst_])
                        kPT = ("p0B.PT", kt, ks_)
                        P.op("scalar", lambda e, idx=idx, sb=sb: e.activation(
                            out=PT[:, idx * 512:(idx + 1) * 512], in_=pst[sb][:],
                            func=AF.Exp, bias=ebias_ap[:, 0:1]),
                            reads=[kst_, ebias_k], writes=[kPT, kst_])
                        if idx > 0:
                            lacc(idx - 1, 0)
                        if idx == 63:
                            lacc(63, 0)
                    else:
                        lacc(idx, 1)
                if half == 0:
                    P.op("scalar", lambda e: e.activation(out=lsb[:], in_=pl[0:1, :],
                                                          func=AF.Identity),
                         reads=[("p0B.l",)], writes=[("p0B.ls",), ("p0B.l",)])
                    P.op("vector", lambda e: e.reciprocal(lrec[:], lsb[:]),
                         reads=[("p0B.ls",)], writes=[("p0B.lrec",)])
                    P.op("tensor", lambda e: e.matmul(
                        pbc[:], ones_rf[:], lrec[:], start=True, stop=True),
                        reads=[("p0B.lrec",), ("p0onesf",)], writes=[("p0B.bc",)])
                    P.op("vector", lambda e: e.tensor_copy(lrecb[:], pbc[:]),
                         reads=[("p0B.bc",)], writes=[("p0B.lrecb",), ("p0B.bc",)])
                for d in range(4):
                    P.op("vector", lambda e, d=d, half=half: e.tensor_mul(
                        oTc[:, (half * 4 + d) * 512:(half * 4 + d) * 512 + 512],
                        pacc[d][:], lrecb[:]),
                        reads=[("p0B.acc", d), ("p0B.lrecb",)],
                        writes=[("p0B.oT",), ("p0B.acc", d)])
            P.dma("qs", oTh[:, qc * 512:(qc + 1) * 512].rearrange(
                "(m p) k -> p m k", p=128), oTc[:],
                  reads=[("p0B.oT",)], writes=[("p0B.oTh", qc)])
            if qc == 1:
                emit_oTg_load(0)
                emit_wo_loads(0)
            elif qc == 3:
                emit_oTg_load(1)

        for och in range(2):
            if och == 1:
                emit_wo_loads(1)
            for gi, (g0, ntile, qcs) in enumerate(grps):
                gb = gi % 2
                kg = ("p0C.oT", gb)
                if (och, gi) not in ((0, 0), (0, 1)):
                    emit_oTg_load(gi)
                for oc in range(och * 4, och * 4 + 4):
                    col0 = oc * 512 - och * 2048
                    for t in range(ntile):
                        P.op("tensor", lambda e, t=t, oc=oc: e.matmul(
                            pc[t][:], ones_r[:], bo_rowb[:, oc * 512:(oc + 1) * 512],
                            start=True, stop=False),
                            reads=[("p0bob",), ("p0ones",)],
                            writes=[("p0C.p", t), bkey[t]])
                    for d in range(8):
                        for t in range(ntile):
                            P.op("tensor", lambda e, d=d, t=t, col0=col0, gb=gb,
                                 ntile=ntile: e.matmul(
                                pc[t][:], oTg[gb][:, d * 128 * ntile + t * 128:
                                                 d * 128 * ntile + t * 128 + 128],
                                wo[d][:, col0:col0 + 512],
                                start=False, stop=(d == 7)),
                                reads=[kg, ("p0C.w", d)],
                                writes=[("p0C.p", t), bkey[t]])
                    for t in range(ntile):
                        b = t % 2
                        kxr = ("p0C.xr", b)
                        row0 = g0 + t * 128
                        P.dma("qs", xr[b][:],
                              xreg[row0:row0 + 128, oc * 512:(oc + 1) * 512],
                              writes=[kxr])
                        kx1 = ("p0C.x1", b)
                        P.op("vector", lambda e, t=t, b=b: e.scalar_tensor_tensor(
                            x1s[b][:], pc[t][:], 0.0, xr[b][:], ALU.add, ALU.add),
                            reads=[("p0C.p", t), kxr],
                            writes=[kx1, ("p0C.p", t)])
                        P.dma("qa", x1[row0:row0 + 128, oc * 512:(oc + 1) * 512],
                              x1s[b][:], reads=[kx1], writes=[("x1h",)])
    P.barrier()



def build_p1(nc, P, ctx, x1_hbm, x2_hbm, xn2_hbm, wblk_hbm):
    E = ctx.enter_context
    wblk_bf = E(nc.sbuf_tensor("p1_wblk_bf", [128, 4 * 4096], BF16))
    sx = [E(nc.sbuf_tensor(f"p1_sx{i}", [128, 4096], F32)) for i in range(2)]
    ftmp = E(nc.sbuf_tensor("p1_ftmp", [128, 4 * 512], F32))
    xf32 = E(nc.sbuf_tensor("p1_xf32", [128, 512], F32))
    xf = [E(nc.sbuf_tensor(f"p1_xf{i}", [128, 512], BF16)) for i in range(2)]
    xfT = [E(nc.sbuf_tensor(f"p1_xfT{i}", [128, 512], BF16)) for i in range(2)]
    etmp = [E(nc.sbuf_tensor(f"p1_et{i}", [128, 512], F32)) for i in range(2)]
    x2t = [E(nc.sbuf_tensor(f"p1_x2t{i}", [128, 4096], F32)) for i in range(2)]
    sxn2 = [E(nc.sbuf_tensor(f"p1_sxn2{i}", [128, 4096], BF16)) for i in range(2)]
    st48 = E(nc.sbuf_tensor("p1_st48", [128, 48], F32))
    st2 = E(nc.sbuf_tensor("p1_st2", [128, 2], F32))
    sd = E(nc.sbuf_tensor("p1_sd", [128, 1], F32))
    rstd = E(nc.sbuf_tensor("p1_rstd", [128, 1], F32))
    nb = E(nc.sbuf_tensor("p1_nb", [128, 1], F32))
    nb8 = E(nc.sbuf_tensor("p1_nb8", [128, 1], F32))
    ident = E(nc.sbuf_tensor("p1_ident", [128, 128], BF16))
    idp = E(nc.sbuf_tensor("p1_idp", [128, 128], F32))
    idf = E(nc.sbuf_tensor("p1_idf", [128, 128], F32))
    pT = [E(nc.psum_tensor(f"p1_pT{i}", [128, 512], BF16)) for i in range(2)]
    pmm = [E(nc.psum_tensor(f"p1_pmm{i}", [128, 512], F32)) for i in range(4)]
    ln_bufs = (st48, st2, sd, rstd, nb)
    kst = ("p1ln", "st")
    eps_ap, eps_k = P.constcol(EPS)

    P.op("gpsimd", lambda e: e.iota(idp[:], pattern=[[0, 128]], channel_multiplier=1,
                                    allow_small_or_imprecise_dtypes=True),
         writes=[("p1", "idp")])
    P.op("gpsimd", lambda e: e.iota(idf[:], pattern=[[1, 128]], channel_multiplier=0,
                                    allow_small_or_imprecise_dtypes=True),
         writes=[("p1", "idf")])
    P.op("vector", lambda e: e.tensor_tensor(ident[:], idp[:], idf[:], ALU.is_equal),
         reads=[("p1", "idp"), ("p1", "idf")], writes=[("p1", "ident")])
    P.dma("qg", wblk_bf[:], wblk_hbm.rearrange("(a p) f -> p a f", p=128),
          writes=[("p1", "wbf")])

    n_tiles = G // 128
    for t in range(n_tiles):
        b = t % 2
        kx = ("p1x", b)
        P.dma("qs", sx[b][:], x1_hbm[t * 128:(t + 1) * 128, :], writes=[kx])
        for j in range(8):
            P.op("vector", lambda e, j=j, b=b: e.bn_stats(
                st48[:, j * 6:(j + 1) * 6], sx[b][:, j * 512:(j + 1) * 512]),
                reads=[kx], writes=[kst])
        P.op("vector", lambda e: e.bn_aggr(st2[:], st48[:]), reads=[kst], writes=[kst])
        P.op("scalar", lambda e: e.activation(out=sd[:], in_=st2[:, 1:2], func=AF.Sqrt,
                                              bias=eps_ap[:, 0:1]),
             reads=[kst, eps_k], writes=[kst])
        P.op("vector", lambda e: e.reciprocal(rstd[:], sd[:]), reads=[kst], writes=[kst])
        P.op("vector", lambda e: e.tensor_scalar(nb8[:], st2[:, 0:1], rstd[:, 0:1],
                                                 -8.0, ALU.mult, ALU.mult),
             reads=[kst], writes=[kst])
        kft = ("p1ft",)
        for j in range(4):
            P.op("vector", lambda e, j=j, b=b: e.tensor_add(
                ftmp[:, j * 512:(j + 1) * 512],
                sx[b][:, (2 * j) * 512:(2 * j) * 512 + 512],
                sx[b][:, (2 * j + 1) * 512:(2 * j + 1) * 512 + 512]),
                reads=[kx], writes=[kft])
        P.op("vector", lambda e: e.tensor_add(ftmp[:, 0:512], ftmp[:, 0:512],
                                              ftmp[:, 512:1024]),
             reads=[kft], writes=[kft])
        P.op("vector", lambda e: e.tensor_add(ftmp[:, 1024:1536], ftmp[:, 1024:1536],
                                              ftmp[:, 1536:2048]),
             reads=[kft], writes=[kft])
        P.op("vector", lambda e: e.tensor_add(xf32[:], ftmp[:, 0:512],
                                              ftmp[:, 1024:1536]),
             reads=[kft], writes=[("p1xf32",)])
        P.op("scalar", lambda e, b=b: e.activation(out=xf[b][:], in_=xf32[:],
                                                   func=AF.Identity,
                                                   bias=nb8[:, 0:1],
                                                   scale=rstd[:, 0:1]),
             reads=[("p1xf32",), kst], writes=[("p1xf", b)])
        for c in range(4):
            P.op("tensor", lambda e, c=c, b=b: e.transpose(
                pT[b][:, c * 128:(c + 1) * 128], xf[b][:, c * 128:(c + 1) * 128],
                ident[:]),
                reads=[("p1xf", b), ("p1", "ident")], writes=[("p1pT", b)])
        P.op("scalar", lambda e, b=b: e.activation(out=xfT[b][:], in_=pT[b][:],
                                                   func=AF.Identity),
             reads=[("p1pT", b)], writes=[("p1xfT", b), ("p1pT", b)])
        for oc in range(8):
            pbi = oc % 4
            kp = ("p1pmm", pbi)
            for c in range(4):
                P.op("tensor", lambda e, oc=oc, c=c, pbi=pbi, b=b: e.matmul(
                    pmm[pbi][:], xfT[b][:, c * 128:(c + 1) * 128],
                    wblk_bf[:, c * 4096 + oc * 512: c * 4096 + oc * 512 + 512],
                    start=(c == 0), stop=(c == 3)),
                    reads=[("p1xfT", b), ("p1", "wbf")], writes=[kp])
            ke = ("p1et", oc % 2)
            P.op("scalar", lambda e, pbi=pbi, oc=oc: e.activation(
                out=etmp[oc % 2][:], in_=pmm[pbi][:], func=AF.Identity),
                reads=[kp], writes=[ke, kp])
            P.op("gpsimd", lambda e, oc=oc, b=b: e.tensor_add(
                x2t[b][:, oc * 512:(oc + 1) * 512], etmp[oc % 2][:],
                sx[b][:, oc * 512:(oc + 1) * 512]),
                reads=[ke, kx], writes=[("p1x2", b)])
        layernorm_tile(P, ctx, nc, "p1ln", x2t[b][:], sxn2[b][:],
                       [("p1x2", b)], [("p1sxn2", b)], ln_bufs)
        P.dma("qa", x2_hbm[t * 128:(t + 1) * 128, :], x2t[b][:],
              reads=[("p1x2", b)], writes=[("x2h",)])
        P.dma("qa", xn2_hbm[t * 128:(t + 1) * 128, :], sxn2[b][:],
              reads=[("p1sxn2", b)], writes=[("p1.xn2", t)])



TWO_PI = 2.0 * math.pi
HALF_PI = 0.5 * math.pi
L1_SCALE = 1.0 / math.sqrt(128.0)


def build_p2(nc, P, ctx, io):
    E = ctx.enter_context
    x2, wqT, wkT, wvT, wo1T = io["x2"], io["wqT"], io["wkT"], io["wvT"], io["wo1T"]
    pos, kmask, amask, invfreq, x3 = (io["pos"], io["kmask"], io["amask"],
                                      io["invfreq"], io["x3"])
    xn2h, qT2h, oT2h, xn3h = io["xn2"], io["qT2"], io["oT2"], io["xn3"]

    st48 = E(nc.sbuf_tensor("p2_st48", [128, 48], F32))
    st2 = E(nc.sbuf_tensor("p2_st2", [128, 2], F32))
    sd = E(nc.sbuf_tensor("p2_sd", [128, 1], F32))
    rstd = E(nc.sbuf_tensor("p2_rstd", [128, 1], F32))
    nbv = E(nc.sbuf_tensor("p2_nb", [128, 1], F32))
    ln_bufs = (st48, st2, sd, rstd, nbv)

    ifq = E(nc.sbuf_tensor("p2_ifq", [128, 1], F32))
    amsk = E(nc.sbuf_tensor("p2_amsk", [128, 1], F32))
    kmsk = E(nc.sbuf_tensor("p2_kmsk", [128, 4], F32))
    ones_rf = E(nc.sbuf_tensor("p2_ones_rf", [1, 128], F32))
    ones_c = E(nc.sbuf_tensor("p2_ones_c", [128, 1], BF16))
    posr = E(nc.sbuf_tensor("p2_posr", [1, G], F32))
    alph = E(nc.sbuf_tensor("p2_alph", [128, 256], F32))
    alph0 = E(nc.sbuf_tensor("p2_alph0", [128, 256], F32))
    omal = E(nc.sbuf_tensor("p2_omal", [128, 256], F32))
    omal0 = E(nc.sbuf_tensor("p2_omal0", [128, 256], F32))
    s_ae = ExitStack()
    krT = s_ae.enter_context(nc.sbuf_tensor("p2_krT", [128, G], BF16))
    vres = s_ae.enter_context(nc.sbuf_tensor("p2_vres", [128, G], BF16))
    s_bc = ExitStack()
    cosT = s_bc.enter_context(nc.sbuf_tensor("p2_cosT", [128, G], F32))
    sinT = s_bc.enter_context(nc.sbuf_tensor("p2_sinT", [128, G], F32))
    s2T = s_bc.enter_context(nc.sbuf_tensor("p2_s2T", [128, G], F32))

    P.dma("qs", ifq[:], invfreq, writes=[("p2ifq",)])
    P.dma("qs", amsk[:], amask, writes=[("p2am",)])
    P.dma("qs", kmsk[:], kmask, writes=[("p2km",)])
    P.dma("qs", posr[:], pos, writes=[("p2pos",)])
    P.op("vector", lambda e: e.memset(ones_rf[:], 1.0), writes=[("p2onesf",)])
    P.op("vector", lambda e: e.memset(ones_c[:], 1.0), writes=[("p2onesc",)])

    with ExitStack() as sB:
        EB = sB.enter_context
        posb = EB(nc.sbuf_tensor("p2b_posb", [128, G], F32))
        tht = EB(nc.sbuf_tensor("p2b_tht", [128, G], F32))
        ki = EB(nc.sbuf_tensor("p2b_ki", [128, G], I32))
        kf = EB(nc.sbuf_tensor("p2b_kf", [128, G], F32))
        thr = EB(nc.sbuf_tensor("p2b_thr", [128, G], F32))
        pbc = EB(nc.psum_tensor("p2b_bc", [128, 512], F32))
        for j in range(5):
            P.op("tensor", lambda e, j=j: e.matmul(
                pbc[:], ones_rf[:], posr[:, j * 512:(j + 1) * 512],
                start=True, stop=True),
                reads=[("p2pos",), ("p2onesf",)], writes=[("p2b.bc",)])
            P.op("vector", lambda e, j=j: e.tensor_copy(
                posb[:, j * 512:(j + 1) * 512], pbc[:]),
                reads=[("p2b.bc",)], writes=[("p2b.posb",), ("p2b.bc",)])
        kth = ("p2b.th",)
        P.op("vector", lambda e: e.tensor_scalar(tht[:], posb[:], ifq[:, 0:1], None,
                                                 ALU.mult),
             reads=[("p2b.posb",), ("p2ifq",)], writes=[kth])
        P.op("vector", lambda e: e.tensor_scalar(kf[:], tht[:], 1.0 / TWO_PI, None,
                                                 ALU.mult), reads=[kth], writes=[("p2b.u",)])
        P.op("vector", lambda e: e.tensor_copy(ki[:], kf[:]),
             reads=[("p2b.u",)], writes=[("p2b.ki",)])
        P.op("vector", lambda e: e.tensor_copy(kf[:], ki[:]),
             reads=[("p2b.ki",)], writes=[("p2b.u",)])
        P.op("vector", lambda e: e.scalar_tensor_tensor(
            thr[:], kf[:], -TWO_PI, tht[:], ALU.mult, ALU.add),
            reads=[("p2b.u",), kth], writes=[("p2b.thr",)])
        P.op("scalar", lambda e: e.activation(out=sinT[:], in_=thr[:], func=AF.Sin),
             reads=[("p2b.thr",)], writes=[("p2sin",)])
        P.op("vector", lambda e: e.tensor_scalar(kf[:], tht[:], 1.0 / TWO_PI,
                                                 0.25, ALU.mult, ALU.add),
             reads=[kth], writes=[("p2b.u",)])
        P.op("vector", lambda e: e.tensor_copy(ki[:], kf[:]),
             reads=[("p2b.u",)], writes=[("p2b.ki",)])
        P.op("vector", lambda e: e.tensor_copy(kf[:], ki[:]),
             reads=[("p2b.ki",)], writes=[("p2b.u",)])
        P.op("vector", lambda e: e.scalar_tensor_tensor(
            thr[:], kf[:], -TWO_PI, tht[:], ALU.mult, ALU.add),
            reads=[("p2b.u",), kth], writes=[("p2b.thr",)])
        P.op("scalar", lambda e: e.activation(out=cosT[:], in_=thr[:], func=AF.Sin,
                                              bias=P.constcol(HALF_PI)[0][:, 0:1]),
             reads=[("p2b.thr",), P.constcol(HALF_PI)[1]], writes=[("p2cos",)])
        P.op("vector", lambda e: e.tensor_copy(s2T[0:64, :], sinT[64:128, :]),
             reads=[("p2sin",)], writes=[("p2s2",)])
        P.op("vector", lambda e: e.tensor_scalar(s2T[64:128, :], sinT[0:64, :],
                                                 -1.0, None, ALU.mult),
             reads=[("p2sin",)], writes=[("p2s2",)])
        P.op("gpsimd", lambda e: e.iota(alph[:], pattern=[[1, 256]],
                                        channel_multiplier=0,
                                        allow_small_or_imprecise_dtypes=True),
             writes=[("p2al",)])
        P.op("vector", lambda e: e.tensor_scalar(alph[:], alph[:], 1.0 / 255.0, None,
                                                 ALU.mult),
             reads=[("p2al",)], writes=[("p2al",)])
        P.op("vector", lambda e: e.tensor_scalar(alph0[:], alph[:], amsk[:, 0:1],
                                                 None, ALU.max),
             reads=[("p2al",), ("p2am",)], writes=[("p2al0",)])
        P.op("vector", lambda e: e.tensor_scalar(omal[:], alph[:], -1.0, 1.0,
                                                 ALU.mult, ALU.add),
             reads=[("p2al",)], writes=[("p2om",)])
        P.op("vector", lambda e: e.tensor_scalar(omal0[:], alph0[:], -1.0, 1.0,
                                                 ALU.mult, ALU.add),
             reads=[("p2al0",)], writes=[("p2om0",)])
    P.barrier()

    with ExitStack() as sC:
        EC = sC.enter_context
        arena = EC(nc.sbuf_tensor("p2c_arena", [128, 32 * 1280], BF16))
        wq = [EC(nc.sbuf_tensor(f"p2c_wq{i}", [128, 32 * 128], BF16)) for i in range(2)]
        wk = EC(nc.sbuf_tensor("p2c_wk", [128, 32 * 128], BF16))
        wv = EC(nc.sbuf_tensor("p2c_wv", [128, 32 * 128], BF16))
        e1 = EC(nc.sbuf_tensor("p2c_e1", [128, 512], F32))
        e2 = EC(nc.sbuf_tensor("p2c_e2", [128, 512], F32))
        qst = [EC(nc.sbuf_tensor(f"p2c_qst{i}", [128, 512], BF16)) for i in range(2)]
        pq = [EC(nc.psum_tensor(f"p2c_pq{i}", [128, 512], F32)) for i in range(2)]
        pv_ = EC(nc.psum_tensor("p2c_pv", [128, 128], F32))

        P.dma("qg", wk[:], wkT.rearrange("p a f -> p (a f)"), writes=[("p2c.wk",)])
        P.dma("qg", wv[:], wvT.rearrange("p a f -> p (a f)"), writes=[("p2c.wv",)])

        def rope_evac(tag, psum_ap, dst_fn, cofs, tn, rkeys, wkeys):
            ke = ("p2c.e",)
            P.op("vector", lambda e: e.tensor_mul(e1[:, 0:tn], psum_ap,
                                                  cosT[:, cofs:cofs + tn]),
                 reads=rkeys + [("p2cos",)], writes=[ke])
            P.op("vector", lambda e: e.tensor_mul(
                e2[0:64, 0:tn], psum_ap[64:128, :], s2T[64:128, cofs:cofs + tn]),
                reads=rkeys + [("p2s2",)], writes=[ke])
            P.op("vector", lambda e: e.tensor_mul(
                e2[64:128, 0:tn], psum_ap[0:64, :], s2T[0:64, cofs:cofs + tn]),
                reads=rkeys + [("p2s2",)], writes=[ke] + [rkeys[0]])
            P.op("vector", lambda e: e.tensor_add(dst_fn[0:128, 0:tn],
                                                  e1[:, 0:tn], e2[:, 0:tn]),
                 reads=[ke], writes=wkeys)

        for ps in range(2):
            r0 = ps * 1280
            for c in range(32):
                P.dma("qs", arena[:, c * 1280:c * 1280 + 1280],
                      xn2h[r0:r0 + 1280, c * 128:(c + 1) * 128], transpose=True,
                      reads=[("p1.xn2", tt) for tt in range(ps * 10, ps * 10 + 10)],
                      writes=[("p2c.ar",)])
            for tc, tn in ((0, 512), (1, 512), (2, 256)):
                kp = ("p2c.qp", 0)
                for c in range(32):
                    P.op("tensor", lambda e, c=c, tc=tc, tn=tn, arena=arena: e.matmul(
                        pq[0][:, 0:tn], wk[:, c * 128:(c + 1) * 128],
                        arena[:, c * 1280 + tc * 512: c * 1280 + tc * 512 + tn],
                        start=(c == 0), stop=(c == 31)),
                        reads=[("p2c.wk",), ("p2c.ar",)], writes=[kp])
                rope_evac("p2c.k", pq[0][:, 0:tn], krT[:, r0 + tc * 512:],
                          r0 + tc * 512, tn, [kp], [("p2krT",)])
            for t in range(10):
                kvp = ("p2c.vp",)
                for c in range(32):
                    P.op("tensor", lambda e, c=c, t=t, arena=arena: e.matmul(
                        pv_[:], arena[:, c * 1280 + t * 128: c * 1280 + t * 128 + 128],
                        wv[:, c * 128:(c + 1) * 128],
                        start=(c == 0), stop=(c == 31)),
                        reads=[("p2c.wv",), ("p2c.ar",)], writes=[kvp])
                P.op("vector", lambda e, t=t, r0=r0: e.tensor_copy(
                    vres[:, r0 + t * 128: r0 + t * 128 + 128], pv_[:]),
                    reads=[kvp], writes=[("p2vres",), kvp])
            q0 = 256 if ps == 0 else 1280
            for m in range(32):
                wb = m % 2
                kw = ("p2c.wq", wb)
                P.dma("qg", wq[wb][:],
                      wqT[m:m + 1].rearrange("o p a f -> p (o a f)"),
                      writes=[kw])
                for tc in range(2):
                    pb = tc % 2
                    kp = ("p2c.qp", pb)
                    c0 = (q0 - r0) + tc * 512
                    for c in range(32):
                        P.op("tensor", lambda e, c=c, c0=c0, wb=wb, pb=pb,
                             arena=arena: e.matmul(
                            pq[pb][:], wq[wb][:, c * 128:(c + 1) * 128],
                            arena[:, c * 1280 + c0: c * 1280 + c0 + 512],
                            start=(c == 0), stop=(c == 31)),
                            reads=[kw, ("p2c.ar",)], writes=[kp])
                    ks = ("p2c.qs", pb)
                    rope_evac("p2c.q", pq[pb][:], qst[pb][:],
                              q0 + tc * 512, 512, [kp], [ks])
                    P.dma("qs", qT2h[m * 128:(m + 1) * 128,
                                     q0 + tc * 512: q0 + tc * 512 + 512],
                          qst[pb][:], reads=[ks], writes=[("p2c.qh",)])
        zt = EC(nc.sbuf_tensor("p2c_zt", [128, 256], BF16))
        P.op("vector", lambda e: e.memset(zt[:], 0.0), writes=[("p2c.z",)])
        for m in range(32):
            P.dma("qs", qT2h[m * 128:(m + 1) * 128, 0:256], zt[:],
                  reads=[("p2c.z",)], writes=[("p2c.qh",)])
            P.dma("qs", qT2h[m * 128:(m + 1) * 128, G - 256:G], zt[:],
                  reads=[("p2c.z",)], writes=[("p2c.qh",)])
    s_bc.close()
    P.barrier()

    s_of = ExitStack()
    oTg = s_of.enter_context(nc.sbuf_tensor("p2f_oT", [128, 32 * 2048], BF16))

    with ExitStack() as sE:
        EE = sE.enter_context
        qrw = [EE(nc.sbuf_tensor(f"p2e_q{i}", [128, 512], BF16)) for i in range(4)]
        PTw = [EE(nc.sbuf_tensor(f"p2e_PT{s}", [128, 4 * 512], BF16)) for s in range(2)]
        wtmp = [EE(nc.sbuf_tensor(f"p2e_wt{s}", [128, 512], F32)) for s in range(2)]
        wn_prev = EE(nc.sbuf_tensor("p2e_wp", [128, 32 * 256], BF16))
        lrec = [EE(nc.sbuf_tensor(f"p2e_lrec{s}", [1, 512], F32)) for s in range(2)]
        lsb = [EE(nc.sbuf_tensor(f"p2e_lsb{s}", [1, 512], F32)) for s in range(2)]
        lrecb = [EE(nc.sbuf_tensor(f"p2e_lrecb{s}", [128, 512], F32)) for s in range(2)]
        bt1 = [EE(nc.sbuf_tensor(f"p2e_bt1{s}", [128, 256], F32)) for s in range(2)]
        bt2 = [EE(nc.sbuf_tensor(f"p2e_bt2{s}", [128, 256], F32)) for s in range(2)]
        pst = [EE(nc.psum_tensor(f"p2e_st{i}", [128, 512], F32)) for i in range(2)]
        pl = [EE(nc.psum_tensor(f"p2e_l{s}", [1, 512], F32)) for s in range(2)]
        pacc = [EE(nc.psum_tensor(f"p2e_acc{s}", [128, 512], F32)) for s in range(2)]
        pbce = [EE(nc.psum_tensor(f"p2e_bc{s}", [128, 512], F32)) for s in range(2)]

        def win_scores(h, w):
            s = h % 2
            qb = h % 4
            kq = ("p2e.q", qb)
            P.dma("qs" if h % 2 == 0 else "qg", qrw[qb][:],
                  qT2h[h * 128:(h + 1) * 128, 256 * w:256 * w + 512],
                  reads=[("p2c.qh",)], writes=[kq])
            for ks_ in range(4):
                sb = (s * 4 + ks_) % 2
                kst_ = ("p2e.st", sb)
                P.op("tensor", lambda e, w=w, ks_=ks_, sb=sb, qb=qb: e.matmul(
                    pst[sb][:], krT[:, w * 256 + ks_ * 128: w * 256 + ks_ * 128 + 128],
                    qrw[qb][:], start=True, stop=True),
                    reads=[("p2krT",), kq], writes=[kst_])
                kPT = ("p2e.PT", s, ks_)
                P.op("scalar", lambda e, s=s, ks_=ks_, sb=sb: e.activation(
                    out=PTw[s][:, ks_ * 512:(ks_ + 1) * 512], in_=pst[sb][:],
                    func=AF.Sigmoid, scale=L1_SCALE),
                    reads=[kst_], writes=[kPT, kst_])
                if w == 8:
                    P.op("vector", lambda e, s=s, ks_=ks_: e.tensor_scalar(
                        PTw[s][:, ks_ * 512:(ks_ + 1) * 512],
                        PTw[s][:, ks_ * 512:(ks_ + 1) * 512],
                        kmsk[:, ks_:ks_ + 1], None, ALU.mult),
                        reads=[kPT, ("p2km",)], writes=[kPT])

        def win_lacc(h, w):
            s = h % 2
            for ks_ in range(4):
                kPT = ("p2e.PT", s, ks_)
                P.op("tensor", lambda e, s=s, ks_=ks_: e.matmul(
                    pl[s][:], ones_c[:], PTw[s][:, ks_ * 512:(ks_ + 1) * 512],
                    start=(ks_ == 0), stop=(ks_ == 3)),
                    reads=[kPT, ("p2onesc",)], writes=[("p2e.l", s)])
                P.op("tensor", lambda e, s=s, w=w, ks_=ks_: e.matmul(
                    pacc[s][:], vres[:, (2 * w + ks_) * 128:(2 * w + ks_) * 128 + 128],
                    PTw[s][:, ks_ * 512:(ks_ + 1) * 512],
                    start=(ks_ == 0), stop=(ks_ == 3)),
                    reads=[("p2vres",), kPT], writes=[("p2e.acc", s)])

        def win_evac(h, w):
            s = h % 2
            P.op("scalar", lambda e, s=s: e.activation(out=lsb[s][:], in_=pl[s][:],
                                                       func=AF.Identity),
                 reads=[("p2e.l", s)], writes=[("p2e.ls", s), ("p2e.l", s)])
            P.op("vector", lambda e, s=s: e.reciprocal(lrec[s][:], lsb[s][:]),
                 reads=[("p2e.ls", s)], writes=[("p2e.lrec", s)])
            P.op("tensor", lambda e, s=s: e.matmul(pbce[s][:], ones_rf[:], lrec[s][:],
                                                   start=True, stop=True),
                 reads=[("p2e.lrec", s), ("p2onesf",)], writes=[("p2e.bc", s)])
            P.op("scalar", lambda e, s=s: e.activation(out=lrecb[s][:], in_=pbce[s][:],
                                                       func=AF.Identity),
                 reads=[("p2e.bc", s)], writes=[("p2e.lrecb", s), ("p2e.bc", s)])
            kwn = ("p2e.wt", s)
            P.op("vector", lambda e, s=s: e.tensor_mul(
                wtmp[s][:], pacc[s][:], lrecb[s][:]),
                reads=[("p2e.acc", s), ("p2e.lrecb", s)],
                writes=[kwn, ("p2e.acc", s)])
            if w >= 1:
                b = w - 1
                al = alph0 if b == 0 else alph
                om = omal0 if b == 0 else omal
                kal = ("p2al0",) if b == 0 else ("p2al",)
                kom = ("p2om0",) if b == 0 else ("p2om",)
                kwp = ("p2e.wp", h)
                P.op("gpsimd", lambda e, s=s, h=h, om=om: e.tensor_mul(
                    bt1[s][:], wn_prev[:, h * 256:(h + 1) * 256], om[:]),
                    reads=[kwp, kom], writes=[("p2e.b1", s)])
                P.op("gpsimd", lambda e, s=s, al=al: e.tensor_mul(
                    bt2[s][:], wtmp[s][:, 0:256], al[:]),
                    reads=[kwn, kal], writes=[("p2e.b2", s)])
                P.op("gpsimd", lambda e, s=s, h=h, b=b: e.tensor_add(
                    oTg[:, h * 2048 + b * 256: h * 2048 + (b + 1) * 256],
                    bt1[s][:], bt2[s][:]),
                    reads=[("p2e.b1", s), ("p2e.b2", s)], writes=[("p2e.ob", h)])
            if w < 8:
                P.op("vector", lambda e, s=s, h=h: e.tensor_copy(
                    wn_prev[:, h * 256:(h + 1) * 256], wtmp[s][:, 256:512]),
                    reads=[kwn], writes=[("p2e.wp", h)])

        for w in range(9):
            for step in range(34):
                if step < 32:
                    win_scores(step, w)
                if 1 <= step:
                    win_lacc(step - 1, w)
                if 2 <= step:
                    win_evac(step - 2, w)
    P.barrier()

    with ExitStack() as sF:
        EF = sF.enter_context
        pan = [EF(nc.sbuf_tensor(f"p2f_pan{i}", [128, 8 * 512], BF16))
               for i in range(2)]
        xr = [EF(nc.sbuf_tensor("p2f_xr0", [128, 512], F32))]
        x3s = [EF(nc.sbuf_tensor("p2f_x3s0", [128, 512], F32))]
        pc = [EF(nc.psum_tensor(f"p2f_p{t}", [128, 512], F32)) for t in range(8)]

        for oc in range(8):
            for th in range(2):
                for f in range(32):
                    wb = (f // 8) % 2
                    kw = ("p2f.w", wb)
                    if th == 0 and f % 8 == 0:
                        chq = f // 8
                        P.dma("qg", pan[wb][:],
                              wo1T[oc:oc + 1, :, chq * 8:(chq + 1) * 8, :].rearrange(
                                  "o p a f -> p (o a f)"),
                              writes=[kw])
                    for t in range(8):
                        P.op("tensor", lambda e, f=f, t=t, th=th, wb=wb, oc=oc: e.matmul(
                            pc[t][:], oTg[:, f * 2048 + th * 1024 + t * 128:
                                          f * 2048 + th * 1024 + t * 128 + 128],
                            pan[wb][:, (f % 8) * 512:(f % 8) * 512 + 512],
                            start=(f == 0), stop=(f == 31)),
                            reads=[("p2e.ob", f), kw], writes=[("p2f.p", t)])
                for t in range(8):
                    b = 0
                    kxr = ("p2f.xr", b)
                    row0 = th * 1024 + t * 128
                    P.dma("qs", xr[b][:],
                          x2[OWN_OFF + row0: OWN_OFF + row0 + 128,
                             oc * 512:(oc + 1) * 512], writes=[kxr])
                    kx3 = ("p2f.x3", b)
                    P.op("vector", lambda e, t=t, b=b: e.scalar_tensor_tensor(
                        x3s[b][:], pc[t][:], 0.0, xr[b][:], ALU.add, ALU.add),
                        reads=[("p2f.p", t), kxr], writes=[kx3, ("p2f.p", t)])
                    P.dma("qs", x3[row0:row0 + 128, oc * 512:(oc + 1) * 512],
                          x3s[b][:], reads=[kx3], writes=[("x3h", th, t)])
        sx3 = EF(nc.sbuf_tensor("p3a_sx", [128, 4096], F32))
        sxn3 = EF(nc.sbuf_tensor("p3a_sxn", [128, 2048], BF16))
        st48a, st2a, sda, rstda, nba = ln_bufs
        eps3, eps3k = P.constcol(EPS)
        kst3 = ("p3aln", "st")
        for tt in range(OWN // 128):
            kx = ("p3a.x",)
            P.dma("qs", sx3[:], x3[tt * 128:(tt + 1) * 128, :],
                  reads=[("x3h", tt // 8, tt % 8)], writes=[kx])
            for j in range(8):
                P.op("vector", lambda e, j=j: e.bn_stats(
                    st48a[:, j * 6:(j + 1) * 6], sx3[:, j * 512:(j + 1) * 512]),
                    reads=[kx], writes=[kst3])
            P.op("vector", lambda e: e.bn_aggr(st2a[:], st48a[:]),
                 reads=[kst3], writes=[kst3])
            P.op("scalar", lambda e: e.activation(out=sda[:], in_=st2a[:, 1:2],
                                                  func=AF.Sqrt, bias=eps3[:, 0:1]),
                 reads=[kst3, eps3k], writes=[kst3])
            P.op("vector", lambda e: e.reciprocal(rstda[:], sda[:]),
                 reads=[kst3], writes=[kst3])
            P.op("vector", lambda e: e.tensor_scalar(nba[:], st2a[:, 0:1],
                                                     rstda[:, 0:1], -1.0,
                                                     ALU.mult, ALU.mult),
                 reads=[kst3], writes=[kst3])
            for hf in range(2):
                kxn = ("p3a.xn",)
                P.op("scalar", lambda e, hf=hf: e.activation(
                    out=sxn3[:], in_=sx3[:, hf * 2048:(hf + 1) * 2048],
                    func=AF.Identity, bias=nba[:, 0:1], scale=rstda[:, 0:1]),
                    reads=[kx, kst3], writes=[kxn])
                P.dma("qa", xn3h[tt * 128:(tt + 1) * 128,
                                 hf * 2048:(hf + 1) * 2048], sxn3[:],
                      reads=[kxn], writes=[("p3a.xnh", tt)])
    s_of.close()
    s_ae.close()
    P.barrier()



def build_p3(nc, P, ctx, io):
    E = ctx.enter_context
    x3, bWT, baseb, rWT, rb = io["x3"], io["bWT"], io["baseb"], io["rWT"], io["rb"]
    Aall, Ball, emap, out_ = io["Aall"], io["Ball"], io["emap"], io["out"]
    xn3h = io["xn3"]

    st48 = E(nc.sbuf_tensor("p3_st48", [128, 48], F32))
    st2 = E(nc.sbuf_tensor("p3_st2", [128, 2], F32))
    sd = E(nc.sbuf_tensor("p3_sd", [128, 1], F32))
    rstd = E(nc.sbuf_tensor("p3_rstd", [128, 1], F32))
    nbv = E(nc.sbuf_tensor("p3_nb", [128, 1], F32))
    ln_bufs = (st48, st2, sd, rstd, nbv)

    identf = E(nc.sbuf_tensor("p3_identf", [128, 128], F32))
    Mb = E(nc.sbuf_tensor("p3_Mb", [15, 8], F32))
    Sel8 = E(nc.sbuf_tensor("p3_Sel8", [8, 128], F32))
    rwb = E(nc.sbuf_tensor("p3_rwb", [128, 32 * 15], BF16))
    ab = E(nc.sbuf_tensor("p3_ab", [128, 32 * 128], BF16))
    ballb = E(nc.sbuf_tensor("p3_ballb", [128, 4096], BF16))
    bbrow = E(nc.sbuf_tensor("p3_bbrow", [1, 4096], BF16))
    rbb = E(nc.sbuf_tensor("p3_rbb", [128, 15], F32))
    rbrow = E(nc.sbuf_tensor("p3_rbrow", [1, 15], F32))
    emapi = E(nc.sbuf_tensor("p3_emapi", [15, 1], I32))
    emapf = E(nc.sbuf_tensor("p3_emapf", [15, 1], F32))
    iot = E(nc.sbuf_tensor("p3_iot", [128, 128], F32))
    iop = E(nc.sbuf_tensor("p3_iop", [128, 1], F32))
    ones_r = E(nc.sbuf_tensor("p3_ones_r", [1, 128], BF16))
    ones_rf = E(nc.sbuf_tensor("p3_ones_rf", [1, 128], F32))

    P.op("vector", lambda e: e.memset(ones_r[:], 1.0), writes=[("p3.ones",)])
    P.op("vector", lambda e: e.memset(ones_rf[:], 1.0), writes=[("p3.onesf",)])
    P.op("gpsimd", lambda e: e.iota(iot[:], pattern=[[1, 128]], channel_multiplier=0,
                                    allow_small_or_imprecise_dtypes=True),
         writes=[("p3.iot",)])
    P.op("gpsimd", lambda e: e.iota(iop[:], pattern=[[0, 1]], channel_multiplier=1,
                                    allow_small_or_imprecise_dtypes=True),
         writes=[("p3.iop",)])
    P.op("vector", lambda e: e.tensor_scalar(identf[:], iot[:], iop[:, 0:1], None,
                                             ALU.is_equal),
         reads=[("p3.iot",), ("p3.iop",)], writes=[("p3.id",)])
    P.dma("qs", emapi[:], emap.rearrange("(p a) -> p a", a=1), writes=[("p3.em",)])
    P.op("vector", lambda e: e.tensor_copy(emapf[:], emapi[:]),
         reads=[("p3.em",)], writes=[("p3.emf",)])
    P.op("vector", lambda e: e.tensor_scalar(Mb[:], iot[0:15, 0:8], emapf[:, 0:1],
                                             2.0, ALU.is_equal, ALU.mult),
         reads=[("p3.iot",), ("p3.emf",)], writes=[("p3.Mb",)])
    P.op("gpsimd", lambda e: e.iota(Sel8[:], pattern=[[0, 16], [1, 8]],
                                    channel_multiplier=0,
                                    allow_small_or_imprecise_dtypes=True),
         writes=[("p3.sel",)])
    P.op("vector", lambda e: e.tensor_scalar(Sel8[:], Sel8[:], iop[0:8, 0:1], None,
                                             ALU.is_equal),
         reads=[("p3.sel",), ("p3.iop",)], writes=[("p3.sel",)])
    P.dma("qg", rwb[:], rWT.rearrange("p a f -> p (a f)"), writes=[("p3.rw",)])
    P.dma("qg", ab[:], Aall.rearrange("p a f -> p (a f)"), writes=[("p3.ab",)])
    P.dma("qg", ballb[:], Ball, writes=[("p3.ball",)])
    P.dma("qg", bbrow[:], baseb.rearrange("(a f) -> a f", a=1), writes=[("p3.bb",)])
    P.dma("qs", rbrow[:], rb.rearrange("(a f) -> a f", a=1), writes=[("p3.rbr",)])

    with ExitStack() as sB:
        EB = sB.enter_context
        arena = EB(nc.sbuf_tensor("p3b_arena", [128, 32 * 2048], BF16))
        hwres = EB(nc.sbuf_tensor("p3b_hw", [128, 2048], BF16))
        lg = EB(nc.sbuf_tensor("p3b_lg", [128, 15], F32))
        nmx = EB(nc.sbuf_tensor("p3b_nmx", [128, 1], F32))
        pex = EB(nc.sbuf_tensor("p3b_pex", [128, 15], F32))
        ssum = EB(nc.sbuf_tensor("p3b_ssum", [128, 1], F32))
        rs = EB(nc.sbuf_tensor("p3b_rs", [128, 1], F32))
        probs = EB(nc.sbuf_tensor("p3b_probs", [128, 15], F32))
        m8 = EB(nc.sbuf_tensor("p3b_m8", [128, 8], F32))
        wk_ = EB(nc.sbuf_tensor("p3b_wk", [128, 15], F32))
        msk = EB(nc.sbuf_tensor("p3b_msk", [128, 15], F32))
        wsel = EB(nc.sbuf_tensor("p3b_wsel", [128, 15], F32))
        wsT = EB(nc.sbuf_tensor("p3b_wsT", [15, 512], F32))
        w8b = EB(nc.sbuf_tensor("p3b_w8b", [8, 512], F32))
        wrepf = EB(nc.sbuf_tensor("p3b_wrepf", [128, 512], F32))

        for rg in range(4):
            for c in range(32):
                P.dma("qs", arena[:, c * 2048 + rg * 512: c * 2048 + rg * 512 + 512],
                      xn3h[rg * 512:(rg + 1) * 512, c * 128:(c + 1) * 128],
                      transpose=True,
                      reads=[("p3a.xnh", tt) for tt in range(rg * 4, rg * 4 + 4)],
                      writes=[("p3.ar", rg)])

        with ExitStack() as sB2:
            EB2 = sB2.enter_context
            plg = EB2(nc.psum_tensor("p3b_plg", [128, 512], F32))
            pwt = EB2(nc.psum_tensor("p3b_pwt", [128, 512], F32))
            pw8 = EB2(nc.psum_tensor("p3b_pw8", [8, 512], F32))
            pwr = EB2(nc.psum_tensor("p3b_pwr", [128, 512], F32))
            ph = EB2(nc.psum_tensor("p3b_ph", [128, 512], F32))
            prb = EB2(nc.psum_tensor("p3b_prb", [128, 512], F32))
            P.op("tensor", lambda e: e.matmul(prb[:, 0:15], ones_rf[:], rbrow[:],
                                              start=True, stop=True),
                 reads=[("p3.rbr",), ("p3.onesf",)], writes=[("p3.prb",)])
            P.op("vector", lambda e: e.tensor_copy(rbb[:], prb[:, 0:15]),
                 reads=[("p3.prb",)], writes=[("p3.rbb",), ("p3.prb",)])

            for grp in range(4):
                for tt in range(4):
                    t = grp * 4 + tt
                    klg = ("p3b.lg",)
                    for c in range(32):
                        P.op("tensor", lambda e, c=c, t=t: e.matmul(
                            plg[:, 0:15],
                            arena[:, c * 2048 + t * 128: c * 2048 + t * 128 + 128],
                            rwb[:, c * 15:(c + 1) * 15],
                            start=(c == 0), stop=(c == 31)),
                            reads=[("p3.ar", grp), ("p3.rw",)], writes=[klg])
                    P.op("vector", lambda e: e.tensor_add(lg[:], plg[:, 0:15], rbb[:]),
                         reads=[klg, ("p3.rbb",)], writes=[("p3b.l2",), klg])
                    P.op("vector", lambda e: e.reduce_max(nmx[:], lg[:], axis=AX.X,
                                                          negate=True),
                         reads=[("p3b.l2",)], writes=[("p3b.nmx",)])
                    P.op("scalar", lambda e: e.activation(out=pex[:], in_=lg[:],
                                                          func=AF.Exp,
                                                          bias=nmx[:, 0:1],
                                                          accum_out=ssum[:]),
                         reads=[("p3b.l2",), ("p3b.nmx",)], writes=[("p3b.pex",)])
                    P.op("vector", lambda e: e.reciprocal(rs[:], ssum[:]),
                         reads=[("p3b.pex",)], writes=[("p3b.rs",)])
                    P.op("vector", lambda e: e.tensor_scalar(probs[:], pex[:],
                                                             rs[:, 0:1], None,
                                                             ALU.mult),
                         reads=[("p3b.pex",), ("p3b.rs",)], writes=[("p3b.pr",)])
                    P.op("vector", lambda e: e.max(m8[:], probs[:]),
                         reads=[("p3b.pr",)], writes=[("p3b.m8",)])
                    P.op("vector", lambda e: e.match_replace(wk_[:], m8[:], probs[:],
                                                             -1.0),
                         reads=[("p3b.m8",), ("p3b.pr",)], writes=[("p3b.wk",)])
                    P.op("vector", lambda e: e.tensor_single_scalar(msk[:], wk_[:],
                                                                    0.0, ALU.is_lt),
                         reads=[("p3b.wk",)], writes=[("p3b.msk",)])
                    P.op("vector", lambda e: e.tensor_mul(wsel[:], msk[:], probs[:]),
                         reads=[("p3b.msk",), ("p3b.pr",)], writes=[("p3b.ws",)])
                    P.op("tensor", lambda e, tt=tt: e.transpose(
                        pwt[0:15, tt * 128:(tt + 1) * 128], wsel[:], identf[:]),
                        reads=[("p3b.ws",), ("p3.id",)], writes=[("p3b.pwt",)])
                P.op("vector", lambda e: e.tensor_copy(wsT[:], pwt[0:15, 0:512]),
                     reads=[("p3b.pwt",)], writes=[("p3b.wsT",), ("p3b.pwt",)])
                P.op("tensor", lambda e: e.matmul(pw8[:], Mb[:], wsT[:],
                                                  start=True, stop=True),
                     reads=[("p3b.wsT",), ("p3.Mb",)], writes=[("p3b.pw8",)])
                P.op("vector", lambda e: e.tensor_copy(w8b[:], pw8[:]),
                     reads=[("p3b.pw8",)], writes=[("p3b.w8b",), ("p3b.pw8",)])
                P.op("tensor", lambda e: e.matmul(pwr[:], Sel8[:], w8b[:],
                                                  start=True, stop=True),
                     reads=[("p3b.w8b",), ("p3.sel",)], writes=[("p3b.pwr",)])
                P.op("vector", lambda e: e.tensor_copy(wrepf[:], pwr[:]),
                     reads=[("p3b.pwr",)], writes=[("p3b.wrf",), ("p3b.pwr",)])
                kh = ("p3b.ph",)
                for c in range(32):
                    P.op("tensor", lambda e, c=c, grp=grp: e.matmul(
                        ph[:], ab[:, c * 128:(c + 1) * 128],
                        arena[:, c * 2048 + grp * 512: c * 2048 + grp * 512 + 512],
                        start=(c == 0), stop=(c == 31)),
                        reads=[("p3.ab",), ("p3.ar", grp)], writes=[kh])
                P.op("vector", lambda e, grp=grp: e.tensor_mul(
                    hwres[:, grp * 512:(grp + 1) * 512], ph[:], wrepf[:]),
                    reads=[kh, ("p3b.wrf",)], writes=[("p3b.hw",), kh])
        with ExitStack() as sC2:
            EC2 = sC2.enter_context
            pan = [EC2(nc.sbuf_tensor(f"p3c_pan{i}", [128, 16 * 512], BF16))
                   for i in range(2)]
            xr = [EC2(nc.sbuf_tensor(f"p3c_xr{i}", [128, 512], F32)) for i in range(2)]
            ost = [EC2(nc.sbuf_tensor(f"p3c_os{i}", [128, 512], F32)) for i in range(2)]
            pc = [EC2(nc.psum_tensor(f"p3c_p{t}", [128, 512], F32)) for t in range(8)]
            for oc in range(8):
                wbs = (0, 1)
                for chf in range(2):
                    P.dma("qg", pan[wbs[chf]][:],
                          bWT[oc:oc + 1, :, chf * 16:(chf + 1) * 16, :].rearrange(
                              "o p a f -> p (o a f)"),
                          writes=[("p3c.w", wbs[chf])])
                for th in range(2):
                    for t in range(8):
                        P.op("tensor", lambda e, t=t, oc=oc: e.matmul(
                            pc[t][:], ones_r[:], bbrow[:, oc * 512:(oc + 1) * 512],
                            start=True, stop=False),
                            reads=[("p3.bb",), ("p3.ones",)], writes=[("p3c.p", t)])
                    for c in range(32):
                        wb = wbs[c // 16]
                        kw = ("p3c.w", wb)
                        for t in range(8):
                            P.op("tensor", lambda e, c=c, t=t, th=th, wb=wb: e.matmul(
                                pc[t][:],
                                arena[:, c * 2048 + th * 1024 + t * 128:
                                      c * 2048 + th * 1024 + t * 128 + 128],
                                pan[wb][:, (c % 16) * 512:(c % 16) * 512 + 512],
                                start=False, stop=False),
                                reads=[("p3.ar", th * 2 + t // 4), kw],
                                writes=[("p3c.p", t)])
                    for t in range(8):
                        P.op("tensor", lambda e, t=t, th=th, oc=oc: e.matmul(
                            pc[t][:],
                            hwres[:, th * 1024 + t * 128: th * 1024 + t * 128 + 128],
                            ballb[:, oc * 512:(oc + 1) * 512],
                            start=False, stop=True),
                            reads=[("p3b.hw",), ("p3.ball",)], writes=[("p3c.p", t)])
                    for t in range(8):
                        b = t % 2
                        kxr = ("p3c.xr", b)
                        row0 = th * 1024 + t * 128
                        P.dma("qs", xr[b][:],
                              x3[row0:row0 + 128, oc * 512:(oc + 1) * 512],
                              writes=[kxr])
                        ko = ("p3c.o", b)
                        P.op("vector", lambda e, t=t, b=b: e.scalar_tensor_tensor(
                            ost[b][:], pc[t][:], 0.0, xr[b][:], ALU.add, ALU.add),
                            reads=[("p3c.p", t), kxr], writes=[ko, ("p3c.p", t)])
                        P.dma("qs", out_[row0:row0 + 128, oc * 512:(oc + 1) * 512],
                              ost[b][:], reads=[ko], writes=[("outh",)])
    P.barrier()



def _region_rows(t0):
    r0 = (t0 - 256) % S
    return (r0 + np.arange(G)) % S, r0


def _build_p1_only(launch_inputs_shape_check=None):
    nc = bass.Bass("TRN2", target_bir_lowering=False, debug=False)
    nc.reset()
    x1 = nc.dram_tensor("x1", [G, NT], F32, kind="ExternalInput").ap()
    wblk = nc.dram_tensor("wblk", [512, NT], I8, kind="ExternalInput").ap()
    x2 = nc.dram_tensor("x2", [G, NT], F32, kind="ExternalOutput").ap()
    with ExitStack() as ctx:
        block = ctx.enter_context(nc.Block())
        P = Prog(nc)
        build_p1(nc, P, ctx, x1, x2, wblk)
        P.barrier()
        P.resolve()
        P.emit(block)
        P._sem_stack.close()
    return nc


def _build_p0_only():
    nc = bass.Bass("TRN2", target_bir_lowering=False, debug=False)
    nc.reset()
    io = {}
    io["xbat"] = nc.dram_tensor("xbat", [S, NT], F32, kind="ExternalInput").ap()
    io["xreg"] = nc.dram_tensor("xreg", [G, NT], F32, kind="ExternalInput").ap()
    io["wqkv"] = nc.dram_tensor("wqkv", [24, 128, 32, 128], F32, kind="ExternalInput").ap()
    io["bqkv"] = nc.dram_tensor("bqkv", [3072], F32, kind="ExternalInput").ap()
    io["wo0"] = nc.dram_tensor("wo0", [1024, NT], F32, kind="ExternalInput").ap()
    io["bo0"] = nc.dram_tensor("bo0", [NT], F32, kind="ExternalInput").ap()
    io["x1"] = nc.dram_tensor("x1", [G, NT], F32, kind="ExternalOutput").ap()
    io["xn"] = nc.dram_tensor("xn_s", [S, NT], BF16, kind="Internal").ap()
    io["xnreg"] = nc.dram_tensor("xnreg_s", [G, NT], BF16, kind="Internal").ap()
    io["kT"] = nc.dram_tensor("kT_s", [1024, S], BF16, kind="Internal").ap()
    io["vT"] = nc.dram_tensor("vT_s", [1024, S], BF16, kind="Internal").ap()
    io["qT"] = nc.dram_tensor("qT_s", [1024, G], BF16, kind="Internal").ap()
    io["oT"] = nc.dram_tensor("oT_s", [1024, G], BF16, kind="Internal").ap()
    io["wkvst"] = nc.dram_tensor("wkvst_s", [2048, NT], BF16, kind="Internal").ap()
    with ExitStack() as ctx:
        block = ctx.enter_context(nc.Block())
        P = Prog(nc)
        build_p0(nc, P, ctx, io)
        P.resolve()
        P.emit(block)
        P._sem_stack.close()
    return nc


_NC_CACHE = {}


def _get_nc(name, builder):
    if name not in _NC_CACHE:
        _NC_CACHE[name] = builder()
    return _NC_CACHE[name]


def _build_fused():
    nc = bass.Bass("TRN2", target_bir_lowering=False, debug=False)
    nc.reset()
    io = {}
    io["xbat"] = nc.dram_tensor("xbat", [S, NT], F32, kind="ExternalInput").ap()
    io["xreg"] = nc.dram_tensor("xreg", [G, NT], F32, kind="ExternalInput").ap()
    io["wqkv"] = nc.dram_tensor("wqkv", [24, 128, 32, 128], F32, kind="ExternalInput").ap()
    io["bqkv"] = nc.dram_tensor("bqkv", [3072], F32, kind="ExternalInput").ap()
    io["wo0"] = nc.dram_tensor("wo0", [1024, NT], F32, kind="ExternalInput").ap()
    io["bo0"] = nc.dram_tensor("bo0", [NT], F32, kind="ExternalInput").ap()
    io["wblk"] = nc.dram_tensor("wblk", [512, NT], I8, kind="ExternalInput").ap()
    io["wqT"] = nc.dram_tensor("wqT", [32, 128, 32, 128], F32, kind="ExternalInput").ap()
    io["wkT"] = nc.dram_tensor("wkT", [128, 32, 128], F32, kind="ExternalInput").ap()
    io["wvT"] = nc.dram_tensor("wvT", [128, 32, 128], F32, kind="ExternalInput").ap()
    io["wo1T"] = nc.dram_tensor("wo1T", [8, 128, 32, 512], F32, kind="ExternalInput").ap()
    io["pos"] = nc.dram_tensor("pos", [1, G], F32, kind="ExternalInput").ap()
    io["kmask"] = nc.dram_tensor("kmask", [128, 4], F32, kind="ExternalInput").ap()
    io["amask"] = nc.dram_tensor("amask", [128, 1], F32, kind="ExternalInput").ap()
    io["invfreq"] = nc.dram_tensor("invfreq", [128, 1], F32, kind="ExternalInput").ap()
    io["bWT"] = nc.dram_tensor("bWT", [8, 128, 32, 512], F32, kind="ExternalInput").ap()
    io["baseb"] = nc.dram_tensor("baseb", [NT], F32, kind="ExternalInput").ap()
    io["rWT"] = nc.dram_tensor("rWT", [128, 32, 15], F32, kind="ExternalInput").ap()
    io["rb"] = nc.dram_tensor("rb", [15], F32, kind="ExternalInput").ap()
    io["Aall"] = nc.dram_tensor("Aall", [128, 32, 128], F32, kind="ExternalInput").ap()
    io["Ball"] = nc.dram_tensor("Ball", [128, NT], F32, kind="ExternalInput").ap()
    io["emap"] = nc.dram_tensor("emap", [15], I32, kind="ExternalInput").ap()
    io["out"] = nc.dram_tensor("out", [OWN, NT], F32, kind="ExternalOutput").ap()
    io["x1"] = nc.dram_tensor("x1_s", [G, NT], F32, kind="Internal").ap()
    io["x2"] = nc.dram_tensor("x2_s", [G, NT], F32, kind="Internal").ap()
    io["x3"] = nc.dram_tensor("x3_s", [OWN, NT], F32, kind="Internal").ap()
    io["xn"] = nc.dram_tensor("xn_s", [S, NT], BF16, kind="Internal").ap()
    io["xnreg"] = nc.dram_tensor("xnreg_s", [G, NT], BF16, kind="Internal").ap()
    io["kT"] = nc.dram_tensor("kT_s", [1024, S], BF16, kind="Internal").ap()
    io["vT"] = nc.dram_tensor("vT_s", [1024, S], BF16, kind="Internal").ap()
    io["qT"] = nc.dram_tensor("qT_s", [1024, G], BF16, kind="Internal").ap()
    io["oT"] = nc.dram_tensor("oT_s", [1024, G], BF16, kind="Internal").ap()
    io["xn2"] = nc.dram_tensor("xn2_s", [G, NT], BF16, kind="Internal").ap()
    io["qT2"] = nc.dram_tensor("qT2_s", [NT, G], BF16, kind="Internal").ap()
    io["oT2"] = nc.dram_tensor("oT2_s", [NT, OWN], BF16, kind="Internal").ap()
    io["xn3"] = nc.dram_tensor("xn3_s", [OWN, NT], BF16, kind="Internal").ap()
    with ExitStack() as ctx:
        block = ctx.enter_context(nc.Block())
        P = Prog(nc)
        with ExitStack() as c0:
            build_p0(nc, P, c0, io)
        with ExitStack() as c1:
            build_p1(nc, P, c1, io["x1"], io["x2"], io["xn2"], io["wblk"])
        P.barrier()
        with ExitStack() as c2:
            build_p2(nc, P, c2, io)
        with ExitStack() as c3:
            build_p3(nc, P, c3, io)
        P.resolve()
        P.emit(block)
        P._sem_stack.close()
    return nc


def kernel(L0_attn_x, L0_attn_Wqkv, L0_attn_bqkv, L0_attn_Wo, L0_attn_bo,
           L0_ffn_kernel, L1_attn_Wq, L1_attn_Wk, L1_attn_Wv, L1_attn_Wo,
           L1_ffn_base_W, L1_ffn_base_b, L1_ffn_router_W, L1_ffn_router_b,
           L1_ffn_A, L1_ffn_B, L1_ffn_expert_map):
    x = np.asarray(L0_attn_x)
    assert x.shape == (2, S, NT)
    ncores = 8

    wqkv = np.ascontiguousarray(
        np.asarray(L0_attn_Wqkv).reshape(32, 128, 24, 128).transpose(2, 1, 0, 3))
    bqkv = np.ascontiguousarray(L0_attn_bqkv)
    wo0 = np.ascontiguousarray(L0_attn_Wo)
    bo0 = np.ascontiguousarray(L0_attn_bo)
    wblk = np.ascontiguousarray(np.asarray(L0_ffn_kernel).reshape(512, NT))
    wqT = np.ascontiguousarray(
        np.asarray(L1_attn_Wq).T.reshape(32, 128, 32, 128).transpose(2, 1, 0, 3))
    wkT = np.ascontiguousarray(
        np.asarray(L1_attn_Wk).T.reshape(32, 128, 128).transpose(1, 0, 2))
    wvT = np.ascontiguousarray(
        np.asarray(L1_attn_Wv).T.reshape(32, 128, 128).transpose(1, 0, 2))
    wo1T = np.ascontiguousarray(
        np.asarray(L1_attn_Wo).T.reshape(32, 128, 8, 512).transpose(2, 1, 0, 3))
    bWT = np.ascontiguousarray(
        np.asarray(L1_ffn_base_W).T.reshape(32, 128, 8, 512).transpose(2, 1, 0, 3))
    baseb = np.ascontiguousarray(L1_ffn_base_b)
    rWT = np.ascontiguousarray(
        np.asarray(L1_ffn_router_W).T.reshape(32, 128, 15).transpose(1, 0, 2))
    rb = np.ascontiguousarray(L1_ffn_router_b)
    Aall = np.ascontiguousarray(
        np.asarray(L1_ffn_A).transpose(1, 0, 2).reshape(128, NT).T
        .reshape(32, 128, 128).transpose(1, 0, 2))
    Ball = np.ascontiguousarray(
        np.asarray(L1_ffn_B).transpose(2, 0, 1).reshape(128, NT))
    emap = np.ascontiguousarray(np.asarray(L1_ffn_expert_map).astype(np.int32))
    invfreq = (1.0 / 10000.0 ** (2.0 * (np.arange(128) // 2) / 128.0)
               ).astype(np.float32).reshape(128, 1)

    in_maps = []
    meta = []
    for c in range(ncores):
        b, t0 = c // 4, (c % 4) * 2048
        rows = (np.arange(G) + (t0 - 256)) % S
        km = np.zeros((128, 4), np.float32)
        for ks in range(4):
            absk = t0 - 256 + 2048 + ks * 128 + np.arange(128)
            km[:, ks] = (absk < S).astype(np.float32)
        in_maps.append(dict(
            xbat=np.ascontiguousarray(x[b]),
            xreg=np.ascontiguousarray(x[b][rows]),
            wqkv=wqkv, bqkv=bqkv, wo0=wo0, bo0=bo0, wblk=wblk,
            wqT=wqT, wkT=wkT, wvT=wvT, wo1T=wo1T,
            pos=rows.astype(np.float32).reshape(1, G), kmask=km,
            amask=np.full((128, 1), 1.0 if t0 == 0 else 0.0, np.float32),
            invfreq=invfreq, bWT=bWT, baseb=baseb, rWT=rWT, rb=rb,
            Aall=Aall, Ball=Ball, emap=emap))
        meta.append((b, t0))

    nc = _get_nc("fused", _build_fused)
    res = run_bass_kernel_spmd(nc, in_maps, core_ids=list(range(ncores)))
    out = np.empty((2, S, NT), np.float32)
    for c in range(ncores):
        b, t0 = meta[c]
        out[b, t0:t0 + OWN] = res.results[c]["out"]
    return out


def _build_p2_only():
    nc = bass.Bass("TRN2", target_bir_lowering=False, debug=False)
    nc.reset()
    io = {}
    io["x2"] = nc.dram_tensor("x2", [G, NT], F32, kind="ExternalInput").ap()
    io["wqT"] = nc.dram_tensor("wqT", [32, 128, 32, 128], F32, kind="ExternalInput").ap()
    io["wkT"] = nc.dram_tensor("wkT", [128, 32, 128], F32, kind="ExternalInput").ap()
    io["wvT"] = nc.dram_tensor("wvT", [128, 32, 128], F32, kind="ExternalInput").ap()
    io["wo1T"] = nc.dram_tensor("wo1T", [8, 128, 32, 512], F32, kind="ExternalInput").ap()
    io["pos"] = nc.dram_tensor("pos", [1, G], F32, kind="ExternalInput").ap()
    io["kmask"] = nc.dram_tensor("kmask", [128, 4], F32, kind="ExternalInput").ap()
    io["amask"] = nc.dram_tensor("amask", [128, 1], F32, kind="ExternalInput").ap()
    io["invfreq"] = nc.dram_tensor("invfreq", [128, 1], F32, kind="ExternalInput").ap()
    io["x3"] = nc.dram_tensor("x3", [OWN, NT], F32, kind="ExternalOutput").ap()
    io["xn2"] = nc.dram_tensor("xn2_s", [G, NT], BF16, kind="Internal").ap()
    io["qT2"] = nc.dram_tensor("qT2_s", [NT, G], BF16, kind="Internal").ap()
    io["oT2"] = nc.dram_tensor("oT2_s", [NT, OWN], BF16, kind="Internal").ap()
    with ExitStack() as ctx:
        block = ctx.enter_context(nc.Block())
        P = Prog(nc)
        build_p2(nc, P, ctx, io)
        P.resolve()
        P.emit(block)
        P._sem_stack.close()
    return nc


def _build_p3_only():
    nc = bass.Bass("TRN2", target_bir_lowering=False, debug=False)
    nc.reset()
    io = {}
    io["x3"] = nc.dram_tensor("x3", [OWN, NT], F32, kind="ExternalInput").ap()
    io["bWT"] = nc.dram_tensor("bWT", [8, 128, 32, 512], F32, kind="ExternalInput").ap()
    io["baseb"] = nc.dram_tensor("baseb", [NT], F32, kind="ExternalInput").ap()
    io["rWT"] = nc.dram_tensor("rWT", [128, 32, 15], F32, kind="ExternalInput").ap()
    io["rb"] = nc.dram_tensor("rb", [15], F32, kind="ExternalInput").ap()
    io["Aall"] = nc.dram_tensor("Aall", [128, 32, 128], F32, kind="ExternalInput").ap()
    io["Ball"] = nc.dram_tensor("Ball", [128, NT], F32, kind="ExternalInput").ap()
    io["emap"] = nc.dram_tensor("emap", [15], I32, kind="ExternalInput").ap()
    io["out"] = nc.dram_tensor("out", [OWN, NT], F32, kind="ExternalOutput").ap()
    io["xn3"] = nc.dram_tensor("xn3_s", [OWN, NT], BF16, kind="Internal").ap()
    with ExitStack() as ctx:
        block = ctx.enter_context(nc.Block())
        P = Prog(nc)
        build_p3(nc, P, ctx, io)
        P.resolve()
        P.emit(block)
        P._sem_stack.close()
    return nc


# version 14
# speedup vs baseline: 1.0612x; 1.0612x over previous
import math
from contextlib import ExitStack

import numpy as np
import ml_dtypes

import concourse.bass as bass
import concourse.mybir as mybir
from concourse.bass_utils import run_bass_kernel_spmd

F32 = mybir.dt.float32
BF16 = mybir.dt.bfloat16
I8 = mybir.dt.int8
I32 = mybir.dt.int32
AF = mybir.ActivationFunctionType
ALU = mybir.AluOpType
AX = mybir.AxisListType

BF = ml_dtypes.bfloat16


ENGINES = ("tensor", "vector", "scalar", "gpsimd", "sync")
DMA_QUEUES = {"qs": "sync", "qa": "scalar", "qg": "gpsimd"}
STREAMS = ENGINES + tuple(DMA_QUEUES)


class _Op:
    __slots__ = ("stream", "fn", "reads", "writes", "idx", "dep", "waits", "inc")

    def __init__(self, stream, fn, reads, writes):
        self.stream = stream
        self.fn = fn
        self.reads = tuple(reads)
        self.writes = tuple(writes)
        self.dep = None
        self.waits = None
        self.inc = False


RELAXED_STREAMS = {"tensor"}


class Prog:
    def __init__(self, nc):
        self.nc = nc
        self.ops = []
        self.stream_ops = {s: [] for s in STREAMS}
        self._consts = {}
        self._const_stack = ExitStack()

    def constcol(self, value):
        if value not in self._consts:
            t = self.nc.alloc_sbuf_tensor(f"cst_{len(self._consts)}", [128, 1], F32, side="right")
            k = ("cst", value)
            self.op("vector", lambda e: e.memset(t[:], float(value)), writes=[k])
            self._consts[value] = (t, k)
        return self._consts[value]

    def op(self, stream, fn, reads=(), writes=()):
        assert stream in STREAMS, stream
        o = _Op(stream, fn, reads, writes)
        o.idx = len(self.stream_ops[stream])
        self.stream_ops[stream].append(o)
        self.ops.append(o)
        return o

    def dma(self, queue, out, in_, reads=(), writes=(), **kw):
        def fn(e):
            return e.dma_start(out=out, in_=in_, **kw)
        return self.op(queue, fn, reads, writes)

    def barrier(self):
        lasts = {s: len(self.stream_ops[s]) - 1 for s in STREAMS
                 if self.stream_ops[s]}
        for eng in ENGINES:
            dep = {s: i for s, i in lasts.items() if s != eng}
            o = self.op(eng, lambda e: e.nop())
            o.dep = dep

    def resolve(self):
        last_w = {}
        readers = {}
        for o in self.ops:
            dep = dict(o.dep) if o.dep is not None else {}

            def add(s, i):
                if s == o.stream and s in RELAXED_STREAMS:
                    return
                if dep.get(s, -1) < i:
                    dep[s] = i
            for k in o.reads:
                if k in last_w:
                    add(*last_w[k])
            for k in o.writes:
                if k in last_w:
                    add(*last_w[k])
                for (s, i) in readers.get(k, ()):
                    add(s, i)
            o.dep = dep
            for k in o.reads:
                readers.setdefault(k, []).append((o.stream, o.idx))
            for k in o.writes:
                last_w[k] = (o.stream, o.idx)
                readers[k] = []
        cur = {s: {} for s in STREAMS}
        snap = {s: [] for s in STREAMS}
        demands = {s: set() for s in STREAMS}
        for o in self.ops:
            g = cur[o.stream]
            waits = []
            for t in sorted(o.dep, key=lambda t: o.dep[t]):
                i = o.dep[t]
                if g.get(t, -1) >= i:
                    continue
                waits.append((t, i))
                demands[t].add(i)
                sn = snap[t][i]
                for t2, i2 in sn.items():
                    if g.get(t2, -1) < i2:
                        g[t2] = i2
                if g.get(t, -1) < i:
                    g[t] = i
            o.waits = waits
            sng = dict(g)
            sng[o.stream] = o.idx
            snap[o.stream].append(sng)
        self.wait_val = {}
        for s in STREAMS:
            step = 16 if s in DMA_QUEUES else 1
            cnt = 0
            for o in self.stream_ops[s]:
                if s in DMA_QUEUES or o.idx in demands[s]:
                    o.inc = True
                    cnt += 1
                self.wait_val[(s, o.idx)] = cnt * step

    def emit(self, block):
        nc = self.nc
        self.sems = {}
        self._sem_stack = ExitStack()
        for s in STREAMS:
            if self.stream_ops[s]:
                self.sems[s] = self._sem_stack.enter_context(
                    nc.semaphore(f"sem_{s}"))

        per_engine = {e: [] for e in ENGINES}
        for o in self.ops:
            eng = DMA_QUEUES.get(o.stream, o.stream)
            per_engine[eng].append(o)

        def runner(items):
            def run(e):
                for o in items:
                    for (t, i) in o.waits:
                        e.wait_ge(self.sems[t], self.wait_val[(t, i)])
                    inst = o.fn(e)
                    if o.inc:
                        step = 16 if o.stream in DMA_QUEUES else 1
                        inst.then_inc(self.sems[o.stream], step)
            return run
        for eng in ENGINES:
            if per_engine[eng]:
                getattr(block, eng)(runner(per_engine[eng]))



NT = 4096
G = 2560
OWN = 2048
OWN_OFF = 256
S = 8192
EPS = 1e-5


def layernorm_tile(P, ctx, nc, name, sx_ap, xn_out_ap, rkeys, wkeys, bufs):
    st48, st2, sd, rstd, nb = bufs
    kst = (name, "st")
    eps_ap, eps_k = P.constcol(EPS)
    for j in range(8):
        P.op("vector", lambda e, j=j: e.bn_stats(st48[:, j * 6:(j + 1) * 6],
                                                 sx_ap[:, j * 512:(j + 1) * 512]),
             reads=rkeys, writes=[kst])
    P.op("vector", lambda e: e.bn_aggr(st2[:], st48[:]), reads=[kst], writes=[kst])
    P.op("scalar", lambda e: e.activation(out=sd[:], in_=st2[:, 1:2], func=AF.Sqrt,
                                          bias=eps_ap[:, 0:1]),
         reads=[kst, eps_k], writes=[kst])
    P.op("vector", lambda e: e.reciprocal(rstd[:], sd[:]), reads=[kst], writes=[kst])
    P.op("vector", lambda e: e.tensor_scalar(nb[:], st2[:, 0:1], rstd[:, 0:1], -1.0,
                                             ALU.mult, ALU.mult),
         reads=[kst], writes=[kst])
    P.op("scalar", lambda e: e.activation(out=xn_out_ap, in_=sx_ap, func=AF.Identity,
                                          bias=nb[:, 0:1], scale=rstd[:, 0:1]),
         reads=list(rkeys) + [kst], writes=wkeys)



P0_EXP_BIAS = -10.0


def build_p0(nc, P, ctx, io):
    E = ctx.enter_context
    xbat, xreg, wqkv, bqkv = io["xbat"], io["xreg"], io["wqkv"], io["bqkv"]
    wo0, bo0, x1 = io["wo0"], io["bo0"], io["x1"]
    xnh, xnregh, kTh, vTh, qTh, oTh = (io["xn"], io["xnreg"], io["kT"],
                                       io["vT"], io["qT"], io["oT"])

    bqc = E(nc.sbuf_tensor("p0_bqc", [128, 24], F32))
    bq32 = E(nc.sbuf_tensor("p0_bq32", [128, 8], F32))
    ones_r = E(nc.sbuf_tensor("p0_ones_r", [1, 128], BF16))
    ones_rf = E(nc.sbuf_tensor("p0_ones_rf", [1, 128], F32))
    ones_c = E(nc.sbuf_tensor("p0_ones_c", [128, 1], BF16))
    st48 = E(nc.sbuf_tensor("p0_st48", [128, 48], F32))
    st2 = E(nc.sbuf_tensor("p0_st2", [128, 2], F32))
    sd = E(nc.sbuf_tensor("p0_sd", [128, 1], F32))
    rstd = E(nc.sbuf_tensor("p0_rstd", [128, 1], F32))
    nbv = E(nc.sbuf_tensor("p0_nb", [128, 1], F32))
    ln_bufs = (st48, st2, sd, rstd, nbv)

    bqkv2 = bqkv.rearrange("(a p) -> a p", p=128)
    for a in range(24):
        P.dma("qs", bqc[:, a:a + 1],
              bqkv2[a:a + 1, :].rearrange("a p -> p a"), writes=[("p0bq",)],
              allow_slow_non_contiguous=True)
    P.op("vector", lambda e: e.tensor_scalar(bq32[:], bqc[:, 0:8], 1.0 / 32.0, None,
                                             ALU.mult),
         reads=[("p0bq",)], writes=[("p0bq32",)])
    P.op("vector", lambda e: e.memset(ones_r[:], 1.0), writes=[("p0ones",)])
    P.op("vector", lambda e: e.memset(ones_rf[:], 1.0), writes=[("p0onesf",)])
    P.op("vector", lambda e: e.memset(ones_c[:], 1.0), writes=[("p0onesc",)])
    ebias_ap, ebias_k = P.constcol(P0_EXP_BIAS)

    with ExitStack() as sA:
        EA = sA.enter_context
        arena = [EA(nc.sbuf_tensor(f"p0A_ar{i}", [128, 32 * 1024], BF16))
                 for i in range(2)]
        sx = [EA(nc.sbuf_tensor(f"p0A_sx{i}", [128, 4096], F32)) for i in range(2)]
        sxn = [EA(nc.sbuf_tensor(f"p0A_sxn{i}", [128, 4096], BF16)) for i in range(2)]
        wkv = [EA(nc.sbuf_tensor(f"p0A_wkv{i}", [128, 32 * 128], BF16)) for i in range(2)]
        kvs = [EA(nc.sbuf_tensor(f"p0A_kvs{i}", [128, 512], BF16)) for i in range(8)]
        pkv = [EA(nc.psum_tensor(f"p0A_pkv{i}", [128, 512], F32)) for i in range(8)]

        secs = []
        rdone = 0
        for s in range(8):
            lst = []
            for t in range(8):
                row0 = s * 1024 + t * 128
                lst.append((xbat[row0:row0 + 128, :], xnh[row0:row0 + 128, :],
                            ("p0A.xnh", s), "qa"))
            nreg = (20 * (s + 1)) // 8 - (20 * s) // 8
            for _ in range(nreg):
                rt = rdone
                rdone += 1
                lst.append((xreg[rt * 128:(rt + 1) * 128, :],
                            xnregh[rt * 128:(rt + 1) * 128, :],
                            ("p0R.xnh",), "qa"))
            secs.append(lst)

        ctr = [0]

        def emit_ln_section(s):
            for (src, dst, key, q) in secs[s]:
                b = ctr[0] % 2
                ctr[0] += 1
                kx = ("p0A.x", b)
                P.dma("qs", sx[b][:], src, writes=[kx])
                kxn = ("p0A.xn", b)
                layernorm_tile(P, ctx, nc, "p0ln", sx[b][:], sxn[b][:], [kx],
                               [kxn], ln_bufs)
                P.dma(q, dst, sxn[b][:], reads=[kxn], writes=[key])

        emit_ln_section(0)
        for ch in range(8):
            ab = ch % 2
            for c in range(32):
                P.dma("qs", arena[ab][:, c * 1024:(c + 1) * 1024],
                      xnh[ch * 1024:ch * 1024 + 1024, c * 128:(c + 1) * 128],
                      transpose=True,
                      reads=[("p0A.xnh", ch)], writes=[("p0A.ar", ab)])
            if ch + 1 < 8:
                emit_ln_section(ch + 1)
            for m in range(16):
                wb = m % 2
                kw = ("p0A.w", wb)
                P.dma("qg", wkv[wb][:],
                      wqkv[8 + m:9 + m].rearrange("o p a f -> p (o a f)"),
                      writes=[kw])
                dsth = kTh if m < 8 else vTh
                dr0 = (m % 8) * 128
                for tc in range(2):
                    pb = (m % 4) * 2 + tc
                    kp = ("p0A.p", pb)
                    for c in range(32):
                        P.op("tensor", lambda e, c=c, tc=tc, wb=wb, pb=pb, ab=ab,
                             arena=arena, wkv=wkv, pkv=pkv: e.matmul(
                            pkv[pb][:], wkv[wb][:, c * 128:(c + 1) * 128],
                            arena[ab][:, c * 1024 + tc * 512:
                                       c * 1024 + tc * 512 + 512],
                            start=(c == 0), stop=(c == 31)),
                            reads=[kw, ("p0A.ar", ab)], writes=[kp])
                    ks = ("p0A.s", pb)
                    P.op("scalar", lambda e, m=m, pb=pb, kvs=kvs, pkv=pkv: e.activation(
                        out=kvs[pb][:], in_=pkv[pb][:], func=AF.Identity,
                        bias=bqc[:, 8 + m:9 + m]),
                        reads=[kp, ("p0bq",)], writes=[ks, kp])
                    P.dma("qa", dsth[dr0:dr0 + 128,
                                     ch * 1024 + tc * 512: ch * 1024 + tc * 512 + 512],
                          kvs[pb][:], reads=[ks], writes=[("p0A.kvh",)])
    P.barrier()

    with ExitStack() as sQ:
        EQ = sQ.enter_context
        arena = EQ(nc.sbuf_tensor("p0Q_arena", [128, 32 * 1280], BF16))
        wq = [EQ(nc.sbuf_tensor(f"p0Q_wq{i}", [128, 32 * 128], BF16)) for i in range(2)]
        qst = [EQ(nc.sbuf_tensor(f"p0Q_qst{i}", [128, 512], BF16)) for i in range(2)]
        pq = [EQ(nc.psum_tensor(f"p0Q_pq{i}", [128, 512], F32)) for i in range(2)]
        for ps in range(2):
            r0 = ps * 1280
            for c in range(32):
                P.dma("qs", arena[:, c * 1280:c * 1280 + 1280],
                      xnregh[r0:r0 + 1280, c * 128:(c + 1) * 128], transpose=True,
                      reads=[("p0R.xnh",)], writes=[("p0Q.ar",)])
            for m in range(8):
                wb = m % 2
                kw = ("p0Q.w", wb)
                P.dma("qg", wq[wb][:],
                      wqkv[m:m + 1].rearrange("o p a f -> p (o a f)"),
                      writes=[kw])
                for tc, tn in ((0, 512), (1, 512), (2, 256)):
                    pb = tc % 2
                    kp = ("p0Q.p", pb)
                    for c in range(32):
                        P.op("tensor", lambda e, c=c, tc=tc, wb=wb, pb=pb, tn=tn,
                             arena=arena, wq=wq, pq=pq: e.matmul(
                            pq[pb][:, 0:tn], wq[wb][:, c * 128:(c + 1) * 128],
                            arena[:, c * 1280 + tc * 512: c * 1280 + tc * 512 + tn],
                            start=(c == 0), stop=(c == 31)),
                            reads=[kw, ("p0Q.ar",)], writes=[kp])
                    ks = ("p0Q.s", pb)
                    P.op("scalar", lambda e, m=m, pb=pb, tn=tn, qst=qst, pq=pq: e.activation(
                        out=qst[pb][:, 0:tn], in_=pq[pb][:, 0:tn], func=AF.Identity,
                        bias=bq32[:, m:m + 1], scale=1.0 / 32.0),
                        reads=[kp, ("p0bq32",)], writes=[ks, kp])
                    P.dma("qs", qTh[m * 128:(m + 1) * 128,
                                    r0 + tc * 512: r0 + tc * 512 + tn],
                          qst[pb][:, 0:tn], reads=[ks], writes=[("p0Q.h",)])
    P.barrier()

    with ExitStack() as sB:
        EB = sB.enter_context
        PT = EB(nc.sbuf_tensor("p0B_PT", [128, 64 * 512], BF16))
        qTc = [EB(nc.sbuf_tensor(f"p0B_qT{i}", [128, 8 * 512], BF16)) for i in range(2)]
        kTt = [EB(nc.sbuf_tensor(f"p0B_kT{i}", [128, 8 * 512], BF16)) for i in range(3)]
        vb = [EB(nc.sbuf_tensor(f"p0B_v{i}", [128, 512], BF16)) for i in range(4)]
        oTc = EB(nc.sbuf_tensor("p0B_oT", [128, 8 * 512], BF16))
        lrec = EB(nc.sbuf_tensor("p0B_lrec", [1, 512], F32))
        lsb = EB(nc.sbuf_tensor("p0B_lsb", [1, 512], F32))
        lrecb = EB(nc.sbuf_tensor("p0B_lrecb", [128, 512], F32))
        pst = [EB(nc.psum_tensor(f"p0B_st{i}", [128, 512], F32)) for i in range(2)]
        pl = EB(nc.psum_tensor("p0B_l", [128, 512], F32))
        pacc = [EB(nc.psum_tensor(f"p0B_acc{i}", [128, 512], F32)) for i in range(4)]
        pbc = EB(nc.psum_tensor("p0B_bc", [128, 512], F32))
        oTg = [EB(nc.sbuf_tensor(f"p0C_oT{i}", [128, 8 * 1024], BF16)) for i in range(2)]
        bo_rowb = EB(nc.sbuf_tensor("p0_bo_rowb", [1, 4096], BF16))
        wo = [EB(nc.sbuf_tensor(f"p0C_wo{d}", [128, 2048], BF16)) for d in range(8)]
        xr = [EB(nc.sbuf_tensor(f"p0C_xr{i}", [128, 512], F32)) for i in range(2)]
        x1s = [EB(nc.sbuf_tensor(f"p0C_x1s{i}", [128, 512], F32)) for i in range(2)]
        pc = [pst[0], pst[1], pacc[0], pacc[1], pacc[2], pacc[3], pbc, pl]
        bkey = [("p0B.st", 0), ("p0B.st", 1), ("p0B.acc", 0), ("p0B.acc", 1),
                ("p0B.acc", 2), ("p0B.acc", 3), ("p0B.bc",), ("p0B.l",)]

        P.dma("qg", bo_rowb[:], bo0.rearrange("(a f) -> a f", a=1), writes=[("p0bob",)])

        grps = [(0, 8, (0, 1)), (1024, 8, (2, 3)), (2048, 4, (4,))]

        def emit_oTg_load(gi):
            g0, ntile, qcs = grps[gi]
            gb = gi % 2
            P.dma("qs", oTg[gb][:, 0:8 * 128 * ntile].rearrange(
                "p (m k) -> p m k", m=8),
                  oTh[:, g0:g0 + ntile * 128].rearrange("(m p) k -> p m k", p=128),
                  reads=[("p0B.oTh", q) for q in qcs], writes=[("p0C.oT", gb)])

        def emit_wo_loads(och):
            for d in range(8):
                P.dma("qg", wo[d][:],
                      wo0[d * 128:(d + 1) * 128, och * 2048:(och + 1) * 2048],
                      writes=[("p0C.w", d)])

        for qc in range(5):
            qb = qc % 2
            kqt = ("p0B.qT", qb)
            P.dma("qs", qTc[qb][:],
                  qTh[:, qc * 512:(qc + 1) * 512].rearrange("(m p) k -> p m k", p=128),
                  reads=[("p0Q.h",)], writes=[kqt])

            def lacc(idx, half):
                kt, ks_ = divmod(idx, 4)
                kPT = ("p0B.PT", kt, ks_)
                vbb = idx % 4
                kv = ("p0B.v", vbb)
                if half == 0:
                    P.op("tensor", lambda e, idx=idx: e.matmul(
                        pl[0:1, :], ones_c[:], PT[:, idx * 512:(idx + 1) * 512],
                        start=(idx == 0), stop=(idx == 63)),
                        reads=[kPT, ("p0onesc",)], writes=[("p0B.l",)])
                for d in range(4):
                    P.op("tensor", lambda e, d=d, idx=idx, vbb=vbb, half=half: e.matmul(
                        pacc[d][:], vb[vbb][:, d * 128:(d + 1) * 128],
                        PT[:, idx * 512:(idx + 1) * 512],
                        start=(idx == 0), stop=(idx == 63)),
                        reads=[kv, kPT], writes=[("p0B.acc", d)])

            for half in range(2):
                for idx in range(64):
                    kt, ks_ = divmod(idx, 4)
                    vbb = idx % 4
                    kv = ("p0B.v", vbb)
                    P.dma("qs", vb[vbb][:],
                          vTh[half * 512:half * 512 + 512, idx * 128:idx * 128 + 128],
                          transpose=True, reads=[("p0A.kvh",)], writes=[kv])
                    if half == 0:
                        kb = kt % 3
                        kk = ("p0B.kT", kb)
                        if ks_ == 0:
                            P.dma("qs" if kt % 2 == 0 else "qg", kTt[kb][:],
                                  kTh[:, kt * 512:(kt + 1) * 512].rearrange(
                                      "(m p) k -> p m k", p=128),
                                  reads=[("p0A.kvh",)], writes=[kk])
                        sb = idx % 2
                        kst_ = ("p0B.st", sb)
                        for d in range(8):
                            P.op("tensor", lambda e, d=d, ks_=ks_, sb=sb, kb=kb,
                                 qb=qb: e.matmul(
                                pst[sb][:],
                                kTt[kb][:, d * 512 + ks_ * 128: d * 512 + ks_ * 128 + 128],
                                qTc[qb][:, d * 512:(d + 1) * 512],
                                start=(d == 0), stop=(d == 7)),
                                reads=[kk, kqt], writes=[kst_])
                        kPT = ("p0B.PT", kt, ks_)
                        P.op("scalar", lambda e, idx=idx, sb=sb: e.activation(
                            out=PT[:, idx * 512:(idx + 1) * 512], in_=pst[sb][:],
                            func=AF.Exp, bias=ebias_ap[:, 0:1]),
                            reads=[kst_, ebias_k], writes=[kPT, kst_])
                        if idx > 0:
                            lacc(idx - 1, 0)
                        if idx == 63:
                            lacc(63, 0)
                    else:
                        lacc(idx, 1)
                if half == 0:
                    P.op("scalar", lambda e: e.activation(out=lsb[:], in_=pl[0:1, :],
                                                          func=AF.Identity),
                         reads=[("p0B.l",)], writes=[("p0B.ls",), ("p0B.l",)])
                    P.op("vector", lambda e: e.reciprocal(lrec[:], lsb[:]),
                         reads=[("p0B.ls",)], writes=[("p0B.lrec",)])
                    P.op("tensor", lambda e: e.matmul(
                        pbc[:], ones_rf[:], lrec[:], start=True, stop=True),
                        reads=[("p0B.lrec",), ("p0onesf",)], writes=[("p0B.bc",)])
                    P.op("vector", lambda e: e.tensor_copy(lrecb[:], pbc[:]),
                         reads=[("p0B.bc",)], writes=[("p0B.lrecb",), ("p0B.bc",)])
                for d in range(4):
                    P.op("vector", lambda e, d=d, half=half: e.tensor_mul(
                        oTc[:, (half * 4 + d) * 512:(half * 4 + d) * 512 + 512],
                        pacc[d][:], lrecb[:]),
                        reads=[("p0B.acc", d), ("p0B.lrecb",)],
                        writes=[("p0B.oT",), ("p0B.acc", d)])
            P.dma("qs", oTh[:, qc * 512:(qc + 1) * 512].rearrange(
                "(m p) k -> p m k", p=128), oTc[:],
                  reads=[("p0B.oT",)], writes=[("p0B.oTh", qc)])
            if qc == 1:
                emit_oTg_load(0)
                emit_wo_loads(0)
            elif qc == 3:
                emit_oTg_load(1)

        for och in range(2):
            if och == 1:
                emit_wo_loads(1)
            for gi, (g0, ntile, qcs) in enumerate(grps):
                gb = gi % 2
                kg = ("p0C.oT", gb)
                if (och, gi) not in ((0, 0), (0, 1)):
                    emit_oTg_load(gi)
                for oc in range(och * 4, och * 4 + 4):
                    col0 = oc * 512 - och * 2048
                    for t in range(ntile):
                        P.op("tensor", lambda e, t=t, oc=oc: e.matmul(
                            pc[t][:], ones_r[:], bo_rowb[:, oc * 512:(oc + 1) * 512],
                            start=True, stop=False),
                            reads=[("p0bob",), ("p0ones",)],
                            writes=[("p0C.p", t), bkey[t]])
                    for d in range(8):
                        for t in range(ntile):
                            P.op("tensor", lambda e, d=d, t=t, col0=col0, gb=gb,
                                 ntile=ntile: e.matmul(
                                pc[t][:], oTg[gb][:, d * 128 * ntile + t * 128:
                                                 d * 128 * ntile + t * 128 + 128],
                                wo[d][:, col0:col0 + 512],
                                start=False, stop=(d == 7)),
                                reads=[kg, ("p0C.w", d)],
                                writes=[("p0C.p", t), bkey[t]])
                    for t in range(ntile):
                        b = t % 2
                        kxr = ("p0C.xr", b)
                        row0 = g0 + t * 128
                        P.dma("qs", xr[b][:],
                              xreg[row0:row0 + 128, oc * 512:(oc + 1) * 512],
                              writes=[kxr])
                        kx1 = ("p0C.x1", b)
                        P.op("vector", lambda e, t=t, b=b: e.scalar_tensor_tensor(
                            x1s[b][:], pc[t][:], 0.0, xr[b][:], ALU.add, ALU.add),
                            reads=[("p0C.p", t), kxr],
                            writes=[kx1, ("p0C.p", t)])
                        P.dma("qa", x1[row0:row0 + 128, oc * 512:(oc + 1) * 512],
                              x1s[b][:], reads=[kx1], writes=[("x1h",)])
    P.barrier()



def build_p1(nc, P, ctx, x1_hbm, x2_hbm, xn2_hbm, wblk_hbm):
    E = ctx.enter_context
    wblk_bf = E(nc.sbuf_tensor("p1_wblk_bf", [128, 4 * 4096], BF16))
    sx = [E(nc.sbuf_tensor(f"p1_sx{i}", [128, 4096], F32)) for i in range(2)]
    ftmp = E(nc.sbuf_tensor("p1_ftmp", [128, 4 * 512], F32))
    xf32 = E(nc.sbuf_tensor("p1_xf32", [128, 512], F32))
    xf = [E(nc.sbuf_tensor(f"p1_xf{i}", [128, 512], BF16)) for i in range(2)]
    xfT = [E(nc.sbuf_tensor(f"p1_xfT{i}", [128, 512], BF16)) for i in range(2)]
    etmp = [E(nc.sbuf_tensor(f"p1_et{i}", [128, 512], F32)) for i in range(2)]
    x2t = [E(nc.sbuf_tensor(f"p1_x2t{i}", [128, 4096], F32)) for i in range(2)]
    sxn2 = [E(nc.sbuf_tensor(f"p1_sxn2{i}", [128, 4096], BF16)) for i in range(2)]
    st48 = E(nc.sbuf_tensor("p1_st48", [128, 48], F32))
    st2 = E(nc.sbuf_tensor("p1_st2", [128, 2], F32))
    sd = E(nc.sbuf_tensor("p1_sd", [128, 1], F32))
    rstd = E(nc.sbuf_tensor("p1_rstd", [128, 1], F32))
    nb = E(nc.sbuf_tensor("p1_nb", [128, 1], F32))
    nb8 = E(nc.sbuf_tensor("p1_nb8", [128, 1], F32))
    ident = E(nc.sbuf_tensor("p1_ident", [128, 128], BF16))
    idp = E(nc.sbuf_tensor("p1_idp", [128, 128], F32))
    idf = E(nc.sbuf_tensor("p1_idf", [128, 128], F32))
    pT = [E(nc.psum_tensor(f"p1_pT{i}", [128, 512], BF16)) for i in range(2)]
    pmm = [E(nc.psum_tensor(f"p1_pmm{i}", [128, 512], F32)) for i in range(4)]
    ln_bufs = (st48, st2, sd, rstd, nb)
    kst = ("p1ln", "st")
    eps_ap, eps_k = P.constcol(EPS)

    P.op("gpsimd", lambda e: e.iota(idp[:], pattern=[[0, 128]], channel_multiplier=1,
                                    allow_small_or_imprecise_dtypes=True),
         writes=[("p1", "idp")])
    P.op("gpsimd", lambda e: e.iota(idf[:], pattern=[[1, 128]], channel_multiplier=0,
                                    allow_small_or_imprecise_dtypes=True),
         writes=[("p1", "idf")])
    P.op("vector", lambda e: e.tensor_tensor(ident[:], idp[:], idf[:], ALU.is_equal),
         reads=[("p1", "idp"), ("p1", "idf")], writes=[("p1", "ident")])
    P.dma("qg", wblk_bf[:], wblk_hbm.rearrange("(a p) f -> p a f", p=128),
          writes=[("p1", "wbf")])

    n_tiles = G // 128
    for t in range(n_tiles):
        b = t % 2
        kx = ("p1x", b)
        P.dma("qs", sx[b][:], x1_hbm[t * 128:(t + 1) * 128, :], writes=[kx])
        for j in range(8):
            P.op("vector", lambda e, j=j, b=b: e.bn_stats(
                st48[:, j * 6:(j + 1) * 6], sx[b][:, j * 512:(j + 1) * 512]),
                reads=[kx], writes=[kst])
        P.op("vector", lambda e: e.bn_aggr(st2[:], st48[:]), reads=[kst], writes=[kst])
        P.op("scalar", lambda e: e.activation(out=sd[:], in_=st2[:, 1:2], func=AF.Sqrt,
                                              bias=eps_ap[:, 0:1]),
             reads=[kst, eps_k], writes=[kst])
        P.op("vector", lambda e: e.reciprocal(rstd[:], sd[:]), reads=[kst], writes=[kst])
        P.op("vector", lambda e: e.tensor_scalar(nb8[:], st2[:, 0:1], rstd[:, 0:1],
                                                 -8.0, ALU.mult, ALU.mult),
             reads=[kst], writes=[kst])
        kft = ("p1ft",)
        for j in range(4):
            P.op("vector", lambda e, j=j, b=b: e.tensor_add(
                ftmp[:, j * 512:(j + 1) * 512],
                sx[b][:, (2 * j) * 512:(2 * j) * 512 + 512],
                sx[b][:, (2 * j + 1) * 512:(2 * j + 1) * 512 + 512]),
                reads=[kx], writes=[kft])
        P.op("vector", lambda e: e.tensor_add(ftmp[:, 0:512], ftmp[:, 0:512],
                                              ftmp[:, 512:1024]),
             reads=[kft], writes=[kft])
        P.op("vector", lambda e: e.tensor_add(ftmp[:, 1024:1536], ftmp[:, 1024:1536],
                                              ftmp[:, 1536:2048]),
             reads=[kft], writes=[kft])
        P.op("vector", lambda e: e.tensor_add(xf32[:], ftmp[:, 0:512],
                                              ftmp[:, 1024:1536]),
             reads=[kft], writes=[("p1xf32",)])
        P.op("scalar", lambda e, b=b: e.activation(out=xf[b][:], in_=xf32[:],
                                                   func=AF.Identity,
                                                   bias=nb8[:, 0:1],
                                                   scale=rstd[:, 0:1]),
             reads=[("p1xf32",), kst], writes=[("p1xf", b)])
        for c in range(4):
            P.op("tensor", lambda e, c=c, b=b: e.transpose(
                pT[b][:, c * 128:(c + 1) * 128], xf[b][:, c * 128:(c + 1) * 128],
                ident[:]),
                reads=[("p1xf", b), ("p1", "ident")], writes=[("p1pT", b)])
        P.op("scalar", lambda e, b=b: e.activation(out=xfT[b][:], in_=pT[b][:],
                                                   func=AF.Identity),
             reads=[("p1pT", b)], writes=[("p1xfT", b), ("p1pT", b)])
        for oc in range(8):
            pbi = oc % 4
            kp = ("p1pmm", pbi)
            for c in range(4):
                P.op("tensor", lambda e, oc=oc, c=c, pbi=pbi, b=b: e.matmul(
                    pmm[pbi][:], xfT[b][:, c * 128:(c + 1) * 128],
                    wblk_bf[:, c * 4096 + oc * 512: c * 4096 + oc * 512 + 512],
                    start=(c == 0), stop=(c == 3)),
                    reads=[("p1xfT", b), ("p1", "wbf")], writes=[kp])
            ke = ("p1et", oc % 2)
            P.op("scalar", lambda e, pbi=pbi, oc=oc: e.activation(
                out=etmp[oc % 2][:], in_=pmm[pbi][:], func=AF.Identity),
                reads=[kp], writes=[ke, kp])
            P.op("gpsimd", lambda e, oc=oc, b=b: e.tensor_add(
                x2t[b][:, oc * 512:(oc + 1) * 512], etmp[oc % 2][:],
                sx[b][:, oc * 512:(oc + 1) * 512]),
                reads=[ke, kx], writes=[("p1x2", b)])
        layernorm_tile(P, ctx, nc, "p1ln", x2t[b][:], sxn2[b][:],
                       [("p1x2", b)], [("p1sxn2", b)], ln_bufs)
        P.dma("qa", x2_hbm[t * 128:(t + 1) * 128, :], x2t[b][:],
              reads=[("p1x2", b)], writes=[("x2h",)])
        P.dma("qa", xn2_hbm[t * 128:(t + 1) * 128, :], sxn2[b][:],
              reads=[("p1sxn2", b)], writes=[("p1.xn2", t)])



TWO_PI = 2.0 * math.pi
HALF_PI = 0.5 * math.pi
L1_SCALE = 1.0 / math.sqrt(128.0)


def build_p2(nc, P, ctx, io):
    E = ctx.enter_context
    x2, wqT, wkT, wvT, wo1T = io["x2"], io["wqT"], io["wkT"], io["wvT"], io["wo1T"]
    pos, kmask, amask, invfreq, x3 = (io["pos"], io["kmask"], io["amask"],
                                      io["invfreq"], io["x3"])
    xn2h, qT2h, oT2h, xn3h = io["xn2"], io["qT2"], io["oT2"], io["xn3"]

    st48 = E(nc.sbuf_tensor("p2_st48", [128, 48], F32))
    st2 = E(nc.sbuf_tensor("p2_st2", [128, 2], F32))
    sd = E(nc.sbuf_tensor("p2_sd", [128, 1], F32))
    rstd = E(nc.sbuf_tensor("p2_rstd", [128, 1], F32))
    nbv = E(nc.sbuf_tensor("p2_nb", [128, 1], F32))
    ln_bufs = (st48, st2, sd, rstd, nbv)

    ifq = E(nc.sbuf_tensor("p2_ifq", [128, 1], F32))
    amsk = E(nc.sbuf_tensor("p2_amsk", [128, 1], F32))
    kmsk = E(nc.sbuf_tensor("p2_kmsk", [128, 4], F32))
    ones_rf = E(nc.sbuf_tensor("p2_ones_rf", [1, 128], F32))
    ones_c = E(nc.sbuf_tensor("p2_ones_c", [128, 1], BF16))
    posr = E(nc.sbuf_tensor("p2_posr", [1, G], F32))
    alph = E(nc.sbuf_tensor("p2_alph", [128, 256], F32))
    alph0 = E(nc.sbuf_tensor("p2_alph0", [128, 256], F32))
    omal = E(nc.sbuf_tensor("p2_omal", [128, 256], F32))
    omal0 = E(nc.sbuf_tensor("p2_omal0", [128, 256], F32))
    s_ae = ExitStack()
    krT = s_ae.enter_context(nc.sbuf_tensor("p2_krT", [128, G], BF16))
    vres = s_ae.enter_context(nc.sbuf_tensor("p2_vres", [128, G], BF16))
    s_bc = ExitStack()
    cosT = s_bc.enter_context(nc.sbuf_tensor("p2_cosT", [128, G], F32))
    sinT = s_bc.enter_context(nc.sbuf_tensor("p2_sinT", [128, G], F32))
    s2T = s_bc.enter_context(nc.sbuf_tensor("p2_s2T", [128, G], F32))

    P.dma("qs", ifq[:], invfreq, writes=[("p2ifq",)])
    P.dma("qs", amsk[:], amask, writes=[("p2am",)])
    P.dma("qs", kmsk[:], kmask, writes=[("p2km",)])
    P.dma("qs", posr[:], pos, writes=[("p2pos",)])
    P.op("vector", lambda e: e.memset(ones_rf[:], 1.0), writes=[("p2onesf",)])
    P.op("vector", lambda e: e.memset(ones_c[:], 1.0), writes=[("p2onesc",)])

    with ExitStack() as sB:
        EB = sB.enter_context
        posb = EB(nc.sbuf_tensor("p2b_posb", [128, G], F32))
        tht = EB(nc.sbuf_tensor("p2b_tht", [128, G], F32))
        ki = EB(nc.sbuf_tensor("p2b_ki", [128, G], I32))
        kf = EB(nc.sbuf_tensor("p2b_kf", [128, G], F32))
        thr = EB(nc.sbuf_tensor("p2b_thr", [128, G], F32))
        pbc = EB(nc.psum_tensor("p2b_bc", [128, 512], F32))
        for j in range(5):
            P.op("tensor", lambda e, j=j: e.matmul(
                pbc[:], ones_rf[:], posr[:, j * 512:(j + 1) * 512],
                start=True, stop=True),
                reads=[("p2pos",), ("p2onesf",)], writes=[("p2b.bc",)])
            P.op("vector", lambda e, j=j: e.tensor_copy(
                posb[:, j * 512:(j + 1) * 512], pbc[:]),
                reads=[("p2b.bc",)], writes=[("p2b.posb",), ("p2b.bc",)])
        kth = ("p2b.th",)
        P.op("vector", lambda e: e.tensor_scalar(tht[:], posb[:], ifq[:, 0:1], None,
                                                 ALU.mult),
             reads=[("p2b.posb",), ("p2ifq",)], writes=[kth])
        P.op("vector", lambda e: e.tensor_scalar(kf[:], tht[:], 1.0 / TWO_PI, None,
                                                 ALU.mult), reads=[kth], writes=[("p2b.u",)])
        P.op("vector", lambda e: e.tensor_copy(ki[:], kf[:]),
             reads=[("p2b.u",)], writes=[("p2b.ki",)])
        P.op("vector", lambda e: e.tensor_copy(kf[:], ki[:]),
             reads=[("p2b.ki",)], writes=[("p2b.u",)])
        P.op("vector", lambda e: e.scalar_tensor_tensor(
            thr[:], kf[:], -TWO_PI, tht[:], ALU.mult, ALU.add),
            reads=[("p2b.u",), kth], writes=[("p2b.thr",)])
        P.op("scalar", lambda e: e.activation(out=sinT[:], in_=thr[:], func=AF.Sin),
             reads=[("p2b.thr",)], writes=[("p2sin",)])
        P.op("vector", lambda e: e.tensor_scalar(kf[:], tht[:], 1.0 / TWO_PI,
                                                 0.25, ALU.mult, ALU.add),
             reads=[kth], writes=[("p2b.u",)])
        P.op("vector", lambda e: e.tensor_copy(ki[:], kf[:]),
             reads=[("p2b.u",)], writes=[("p2b.ki",)])
        P.op("vector", lambda e: e.tensor_copy(kf[:], ki[:]),
             reads=[("p2b.ki",)], writes=[("p2b.u",)])
        P.op("vector", lambda e: e.scalar_tensor_tensor(
            thr[:], kf[:], -TWO_PI, tht[:], ALU.mult, ALU.add),
            reads=[("p2b.u",), kth], writes=[("p2b.thr",)])
        P.op("scalar", lambda e: e.activation(out=cosT[:], in_=thr[:], func=AF.Sin,
                                              bias=P.constcol(HALF_PI)[0][:, 0:1]),
             reads=[("p2b.thr",), P.constcol(HALF_PI)[1]], writes=[("p2cos",)])
        P.op("vector", lambda e: e.tensor_copy(s2T[0:64, :], sinT[64:128, :]),
             reads=[("p2sin",)], writes=[("p2s2",)])
        P.op("vector", lambda e: e.tensor_scalar(s2T[64:128, :], sinT[0:64, :],
                                                 -1.0, None, ALU.mult),
             reads=[("p2sin",)], writes=[("p2s2",)])
        P.op("gpsimd", lambda e: e.iota(alph[:], pattern=[[1, 256]],
                                        channel_multiplier=0,
                                        allow_small_or_imprecise_dtypes=True),
             writes=[("p2al",)])
        P.op("vector", lambda e: e.tensor_scalar(alph[:], alph[:], 1.0 / 255.0, None,
                                                 ALU.mult),
             reads=[("p2al",)], writes=[("p2al",)])
        P.op("vector", lambda e: e.tensor_scalar(alph0[:], alph[:], amsk[:, 0:1],
                                                 None, ALU.max),
             reads=[("p2al",), ("p2am",)], writes=[("p2al0",)])
        P.op("vector", lambda e: e.tensor_scalar(omal[:], alph[:], -1.0, 1.0,
                                                 ALU.mult, ALU.add),
             reads=[("p2al",)], writes=[("p2om",)])
        P.op("vector", lambda e: e.tensor_scalar(omal0[:], alph0[:], -1.0, 1.0,
                                                 ALU.mult, ALU.add),
             reads=[("p2al0",)], writes=[("p2om0",)])
    P.barrier()

    with ExitStack() as sC:
        EC = sC.enter_context
        arena = EC(nc.sbuf_tensor("p2c_arena", [128, 32 * 1280], BF16))
        wq = [EC(nc.sbuf_tensor(f"p2c_wq{i}", [128, 32 * 128], BF16)) for i in range(2)]
        wk = EC(nc.sbuf_tensor("p2c_wk", [128, 32 * 128], BF16))
        wv = EC(nc.sbuf_tensor("p2c_wv", [128, 32 * 128], BF16))
        e1 = EC(nc.sbuf_tensor("p2c_e1", [128, 512], F32))
        e2 = EC(nc.sbuf_tensor("p2c_e2", [128, 512], F32))
        qst = [EC(nc.sbuf_tensor(f"p2c_qst{i}", [128, 512], BF16)) for i in range(2)]
        pq = [EC(nc.psum_tensor(f"p2c_pq{i}", [128, 512], F32)) for i in range(2)]
        pv_ = EC(nc.psum_tensor("p2c_pv", [128, 128], F32))

        P.dma("qg", wk[:], wkT.rearrange("p a f -> p (a f)"), writes=[("p2c.wk",)])
        P.dma("qg", wv[:], wvT.rearrange("p a f -> p (a f)"), writes=[("p2c.wv",)])

        def rope_evac(tag, psum_ap, dst_fn, cofs, tn, rkeys, wkeys):
            ke = ("p2c.e",)
            P.op("vector", lambda e: e.tensor_mul(e1[:, 0:tn], psum_ap,
                                                  cosT[:, cofs:cofs + tn]),
                 reads=rkeys + [("p2cos",)], writes=[ke])
            P.op("vector", lambda e: e.tensor_mul(
                e2[0:64, 0:tn], psum_ap[64:128, :], s2T[64:128, cofs:cofs + tn]),
                reads=rkeys + [("p2s2",)], writes=[ke])
            P.op("vector", lambda e: e.tensor_mul(
                e2[64:128, 0:tn], psum_ap[0:64, :], s2T[0:64, cofs:cofs + tn]),
                reads=rkeys + [("p2s2",)], writes=[ke] + [rkeys[0]])
            P.op("vector", lambda e: e.tensor_add(dst_fn[0:128, 0:tn],
                                                  e1[:, 0:tn], e2[:, 0:tn]),
                 reads=[ke], writes=wkeys)

        for ps in range(2):
            r0 = ps * 1280
            for c in range(32):
                P.dma("qs", arena[:, c * 1280:c * 1280 + 1280],
                      xn2h[r0:r0 + 1280, c * 128:(c + 1) * 128], transpose=True,
                      reads=[("p1.xn2", tt) for tt in range(ps * 10, ps * 10 + 10)],
                      writes=[("p2c.ar",)])
            for tc, tn in ((0, 512), (1, 512), (2, 256)):
                kp = ("p2c.qp", 0)
                for c in range(32):
                    P.op("tensor", lambda e, c=c, tc=tc, tn=tn, arena=arena: e.matmul(
                        pq[0][:, 0:tn], wk[:, c * 128:(c + 1) * 128],
                        arena[:, c * 1280 + tc * 512: c * 1280 + tc * 512 + tn],
                        start=(c == 0), stop=(c == 31)),
                        reads=[("p2c.wk",), ("p2c.ar",)], writes=[kp])
                rope_evac("p2c.k", pq[0][:, 0:tn], krT[:, r0 + tc * 512:],
                          r0 + tc * 512, tn, [kp], [("p2krT",)])
            for t in range(10):
                kvp = ("p2c.vp",)
                for c in range(32):
                    P.op("tensor", lambda e, c=c, t=t, arena=arena: e.matmul(
                        pv_[:], arena[:, c * 1280 + t * 128: c * 1280 + t * 128 + 128],
                        wv[:, c * 128:(c + 1) * 128],
                        start=(c == 0), stop=(c == 31)),
                        reads=[("p2c.wv",), ("p2c.ar",)], writes=[kvp])
                P.op("vector", lambda e, t=t, r0=r0: e.tensor_copy(
                    vres[:, r0 + t * 128: r0 + t * 128 + 128], pv_[:]),
                    reads=[kvp], writes=[("p2vres",), kvp])
            q0 = 256 if ps == 0 else 1280
            for m in range(32):
                wb = m % 2
                kw = ("p2c.wq", wb)
                P.dma("qg", wq[wb][:],
                      wqT[m:m + 1].rearrange("o p a f -> p (o a f)"),
                      writes=[kw])
                for tc in range(2):
                    pb = tc % 2
                    kp = ("p2c.qp", pb)
                    c0 = (q0 - r0) + tc * 512
                    for c in range(32):
                        P.op("tensor", lambda e, c=c, c0=c0, wb=wb, pb=pb,
                             arena=arena: e.matmul(
                            pq[pb][:], wq[wb][:, c * 128:(c + 1) * 128],
                            arena[:, c * 1280 + c0: c * 1280 + c0 + 512],
                            start=(c == 0), stop=(c == 31)),
                            reads=[kw, ("p2c.ar",)], writes=[kp])
                    ks = ("p2c.qs", pb)
                    rope_evac("p2c.q", pq[pb][:], qst[pb][:],
                              q0 + tc * 512, 512, [kp], [ks])
                    P.dma("qs", qT2h[m * 128:(m + 1) * 128,
                                     q0 + tc * 512: q0 + tc * 512 + 512],
                          qst[pb][:], reads=[ks], writes=[("p2c.qh",)])
        zt = EC(nc.sbuf_tensor("p2c_zt", [128, 256], BF16))
        P.op("vector", lambda e: e.memset(zt[:], 0.0), writes=[("p2c.z",)])
        for m in range(32):
            P.dma("qs", qT2h[m * 128:(m + 1) * 128, 0:256], zt[:],
                  reads=[("p2c.z",)], writes=[("p2c.qh",)])
            P.dma("qs", qT2h[m * 128:(m + 1) * 128, G - 256:G], zt[:],
                  reads=[("p2c.z",)], writes=[("p2c.qh",)])
    s_bc.close()
    P.barrier()

    s_of = ExitStack()
    oTg = s_of.enter_context(nc.sbuf_tensor("p2f_oT", [128, 32 * 2048], BF16))

    with ExitStack() as sE:
        EE = sE.enter_context
        qrw = [EE(nc.sbuf_tensor(f"p2e_q{i}", [128, 512], BF16)) for i in range(4)]
        PTw = [EE(nc.sbuf_tensor(f"p2e_PT{s}", [128, 4 * 512], BF16)) for s in range(2)]
        wtmp = [EE(nc.sbuf_tensor(f"p2e_wt{s}", [128, 512], F32)) for s in range(2)]
        wn_prev = EE(nc.sbuf_tensor("p2e_wp", [128, 32 * 256], BF16))
        lrec = [EE(nc.sbuf_tensor(f"p2e_lrec{s}", [1, 512], F32)) for s in range(2)]
        lsb = [EE(nc.sbuf_tensor(f"p2e_lsb{s}", [1, 512], F32)) for s in range(2)]
        lrecb = [EE(nc.sbuf_tensor(f"p2e_lrecb{s}", [128, 512], F32)) for s in range(2)]
        bt1 = [EE(nc.sbuf_tensor(f"p2e_bt1{s}", [128, 256], F32)) for s in range(2)]
        bt2 = [EE(nc.sbuf_tensor(f"p2e_bt2{s}", [128, 256], F32)) for s in range(2)]
        pst = [EE(nc.psum_tensor(f"p2e_st{i}", [128, 512], F32)) for i in range(3)]
        pl = [EE(nc.psum_tensor(f"p2e_l{s}", [1, 512], F32)) for s in range(2)]
        pacc = [EE(nc.psum_tensor(f"p2e_acc{s}", [128, 512], F32)) for s in range(2)]
        pbce = EE(nc.psum_tensor("p2e_bc", [128, 512], F32))

        def emit_qload(h, w):
            qb = h % 4
            P.dma("qs" if h % 2 == 0 else "qg", qrw[qb][:],
                  qT2h[h * 128:(h + 1) * 128, 256 * w:256 * w + 512],
                  reads=[("p2c.qh",)], writes=[("p2e.q", qb)])

        def win_scores(h, w):
            s = h % 2
            qb = h % 4
            kq = ("p2e.q", qb)
            for ks_ in range(4):
                sb = (h * 4 + ks_) % 3
                kst_ = ("p2e.st", sb)
                P.op("tensor", lambda e, w=w, ks_=ks_, sb=sb, qb=qb: e.matmul(
                    pst[sb][:], krT[:, w * 256 + ks_ * 128: w * 256 + ks_ * 128 + 128],
                    qrw[qb][:], start=True, stop=True),
                    reads=[("p2krT",), kq], writes=[kst_])
                kPT = ("p2e.PT", s, ks_)
                P.op("scalar", lambda e, s=s, ks_=ks_, sb=sb: e.activation(
                    out=PTw[s][:, ks_ * 512:(ks_ + 1) * 512], in_=pst[sb][:],
                    func=AF.Sigmoid, scale=L1_SCALE),
                    reads=[kst_], writes=[kPT, kst_])
                if w == 8:
                    P.op("vector", lambda e, s=s, ks_=ks_: e.tensor_scalar(
                        PTw[s][:, ks_ * 512:(ks_ + 1) * 512],
                        PTw[s][:, ks_ * 512:(ks_ + 1) * 512],
                        kmsk[:, ks_:ks_ + 1], None, ALU.mult),
                        reads=[kPT, ("p2km",)], writes=[kPT])

        def win_lacc(h, w):
            s = h % 2
            for ks_ in range(4):
                kPT = ("p2e.PT", s, ks_)
                P.op("tensor", lambda e, s=s, ks_=ks_: e.matmul(
                    pl[s][:], ones_c[:], PTw[s][:, ks_ * 512:(ks_ + 1) * 512],
                    start=(ks_ == 0), stop=(ks_ == 3)),
                    reads=[kPT, ("p2onesc",)], writes=[("p2e.l", s)])
            for ks_ in range(4):
                kPT = ("p2e.PT", s, ks_)
                P.op("tensor", lambda e, s=s, w=w, ks_=ks_: e.matmul(
                    pacc[s][:], vres[:, (2 * w + ks_) * 128:(2 * w + ks_) * 128 + 128],
                    PTw[s][:, ks_ * 512:(ks_ + 1) * 512],
                    start=(ks_ == 0), stop=(ks_ == 3)),
                    reads=[("p2vres",), kPT], writes=[("p2e.acc", s)])

        def win_evac(h, w):
            s = h % 2
            P.op("scalar", lambda e, s=s: e.activation(out=lsb[s][:], in_=pl[s][:],
                                                       func=AF.Identity),
                 reads=[("p2e.l", s)], writes=[("p2e.ls", s), ("p2e.l", s)])
            P.op("vector", lambda e, s=s: e.reciprocal(lrec[s][:], lsb[s][:]),
                 reads=[("p2e.ls", s)], writes=[("p2e.lrec", s)])
            P.op("tensor", lambda e, s=s: e.matmul(pbce[:], ones_rf[:], lrec[s][:],
                                                   start=True, stop=True),
                 reads=[("p2e.lrec", s), ("p2onesf",)], writes=[("p2e.bc",)])
            P.op("vector", lambda e, s=s: e.tensor_copy(lrecb[s][:], pbce[:]),
                 reads=[("p2e.bc",)], writes=[("p2e.lrecb", s), ("p2e.bc",)])
            kwn = ("p2e.wt", s)
            P.op("vector", lambda e, s=s: e.tensor_mul(
                wtmp[s][:], pacc[s][:], lrecb[s][:]),
                reads=[("p2e.acc", s), ("p2e.lrecb", s)],
                writes=[kwn, ("p2e.acc", s)])
            if w >= 1:
                b = w - 1
                al = alph0 if b == 0 else alph
                om = omal0 if b == 0 else omal
                kal = ("p2al0",) if b == 0 else ("p2al",)
                kom = ("p2om0",) if b == 0 else ("p2om",)
                kwp = ("p2e.wp", h)
                P.op("gpsimd", lambda e, s=s, h=h, om=om: e.tensor_mul(
                    bt1[s][:], wn_prev[:, h * 256:(h + 1) * 256], om[:]),
                    reads=[kwp, kom], writes=[("p2e.b1", s)])
                P.op("gpsimd", lambda e, s=s, al=al: e.tensor_mul(
                    bt2[s][:], wtmp[s][:, 0:256], al[:]),
                    reads=[kwn, kal], writes=[("p2e.b2", s)])
                P.op("gpsimd", lambda e, s=s, h=h, b=b: e.tensor_add(
                    oTg[:, h * 2048 + b * 256: h * 2048 + (b + 1) * 256],
                    bt1[s][:], bt2[s][:]),
                    reads=[("p2e.b1", s), ("p2e.b2", s)], writes=[("p2e.ob", h)])
            if w < 8:
                P.op("vector", lambda e, s=s, h=h: e.tensor_copy(
                    wn_prev[:, h * 256:(h + 1) * 256], wtmp[s][:, 256:512]),
                    reads=[kwn], writes=[("p2e.wp", h)])

        emit_qload(0, 0)
        emit_qload(1, 0)
        for w in range(9):
            for step in range(34):
                if step + 2 < 32:
                    emit_qload(step + 2, w)
                elif w + 1 < 9 and step + 2 in (32, 33):
                    emit_qload(step + 2 - 32, w + 1)
                if step < 32:
                    win_scores(step, w)
                if 1 <= step:
                    win_lacc(step - 1, w)
                if 2 <= step:
                    win_evac(step - 2, w)
    P.barrier()

    with ExitStack() as sF:
        EF = sF.enter_context
        pan = [EF(nc.sbuf_tensor(f"p2f_pan{i}", [128, 8 * 512], BF16))
               for i in range(2)]
        xr = [EF(nc.sbuf_tensor("p2f_xr0", [128, 512], F32))]
        x3s = [EF(nc.sbuf_tensor("p2f_x3s0", [128, 512], F32))]
        pc = [EF(nc.psum_tensor(f"p2f_p{t}", [128, 512], F32)) for t in range(8)]

        for oc in range(8):
            for th in range(2):
                for f in range(32):
                    wb = (f // 8) % 2
                    kw = ("p2f.w", wb)
                    if th == 0 and f % 8 == 0:
                        chq = f // 8
                        P.dma("qg", pan[wb][:],
                              wo1T[oc:oc + 1, :, chq * 8:(chq + 1) * 8, :].rearrange(
                                  "o p a f -> p (o a f)"),
                              writes=[kw])
                    for t in range(8):
                        P.op("tensor", lambda e, f=f, t=t, th=th, wb=wb, oc=oc: e.matmul(
                            pc[t][:], oTg[:, f * 2048 + th * 1024 + t * 128:
                                          f * 2048 + th * 1024 + t * 128 + 128],
                            pan[wb][:, (f % 8) * 512:(f % 8) * 512 + 512],
                            start=(f == 0), stop=(f == 31)),
                            reads=[("p2e.ob", f), kw], writes=[("p2f.p", t)])
                for t in range(8):
                    b = 0
                    kxr = ("p2f.xr", b)
                    row0 = th * 1024 + t * 128
                    P.dma("qs", xr[b][:],
                          x2[OWN_OFF + row0: OWN_OFF + row0 + 128,
                             oc * 512:(oc + 1) * 512], writes=[kxr])
                    kx3 = ("p2f.x3", b)
                    P.op("vector", lambda e, t=t, b=b: e.scalar_tensor_tensor(
                        x3s[b][:], pc[t][:], 0.0, xr[b][:], ALU.add, ALU.add),
                        reads=[("p2f.p", t), kxr], writes=[kx3, ("p2f.p", t)])
                    P.dma("qs", x3[row0:row0 + 128, oc * 512:(oc + 1) * 512],
                          x3s[b][:], reads=[kx3], writes=[("x3h", th, t)])
        sx3 = EF(nc.sbuf_tensor("p3a_sx", [128, 4096], F32))
        sxn3 = EF(nc.sbuf_tensor("p3a_sxn", [128, 2048], BF16))
        st48a, st2a, sda, rstda, nba = ln_bufs
        eps3, eps3k = P.constcol(EPS)
        kst3 = ("p3aln", "st")
        for tt in range(OWN // 128):
            kx = ("p3a.x",)
            P.dma("qs", sx3[:], x3[tt * 128:(tt + 1) * 128, :],
                  reads=[("x3h", tt // 8, tt % 8)], writes=[kx])
            for j in range(8):
                P.op("vector", lambda e, j=j: e.bn_stats(
                    st48a[:, j * 6:(j + 1) * 6], sx3[:, j * 512:(j + 1) * 512]),
                    reads=[kx], writes=[kst3])
            P.op("vector", lambda e: e.bn_aggr(st2a[:], st48a[:]),
                 reads=[kst3], writes=[kst3])
            P.op("scalar", lambda e: e.activation(out=sda[:], in_=st2a[:, 1:2],
                                                  func=AF.Sqrt, bias=eps3[:, 0:1]),
                 reads=[kst3, eps3k], writes=[kst3])
            P.op("vector", lambda e: e.reciprocal(rstda[:], sda[:]),
                 reads=[kst3], writes=[kst3])
            P.op("vector", lambda e: e.tensor_scalar(nba[:], st2a[:, 0:1],
                                                     rstda[:, 0:1], -1.0,
                                                     ALU.mult, ALU.mult),
                 reads=[kst3], writes=[kst3])
            for hf in range(2):
                kxn = ("p3a.xn",)
                P.op("scalar", lambda e, hf=hf: e.activation(
                    out=sxn3[:], in_=sx3[:, hf * 2048:(hf + 1) * 2048],
                    func=AF.Identity, bias=nba[:, 0:1], scale=rstda[:, 0:1]),
                    reads=[kx, kst3], writes=[kxn])
                P.dma("qa", xn3h[tt * 128:(tt + 1) * 128,
                                 hf * 2048:(hf + 1) * 2048], sxn3[:],
                      reads=[kxn], writes=[("p3a.xnh", tt)])
    s_of.close()
    s_ae.close()
    P.barrier()



def build_p3(nc, P, ctx, io):
    E = ctx.enter_context
    x3, bWT, baseb, rWT, rb = io["x3"], io["bWT"], io["baseb"], io["rWT"], io["rb"]
    Aall, Ball, emap, out_ = io["Aall"], io["Ball"], io["emap"], io["out"]
    xn3h = io["xn3"]

    st48 = E(nc.sbuf_tensor("p3_st48", [128, 48], F32))
    st2 = E(nc.sbuf_tensor("p3_st2", [128, 2], F32))
    sd = E(nc.sbuf_tensor("p3_sd", [128, 1], F32))
    rstd = E(nc.sbuf_tensor("p3_rstd", [128, 1], F32))
    nbv = E(nc.sbuf_tensor("p3_nb", [128, 1], F32))
    ln_bufs = (st48, st2, sd, rstd, nbv)

    identf = E(nc.sbuf_tensor("p3_identf", [128, 128], F32))
    Mb = E(nc.sbuf_tensor("p3_Mb", [15, 8], F32))
    Sel8 = E(nc.sbuf_tensor("p3_Sel8", [8, 128], F32))
    rwb = E(nc.sbuf_tensor("p3_rwb", [128, 32 * 15], BF16))
    ab = E(nc.sbuf_tensor("p3_ab", [128, 32 * 128], BF16))
    ballb = E(nc.sbuf_tensor("p3_ballb", [128, 4096], BF16))
    bbrow = E(nc.sbuf_tensor("p3_bbrow", [1, 4096], BF16))
    rbb = E(nc.sbuf_tensor("p3_rbb", [128, 15], F32))
    rbrow = E(nc.sbuf_tensor("p3_rbrow", [1, 15], F32))
    emapi = E(nc.sbuf_tensor("p3_emapi", [15, 1], I32))
    emapf = E(nc.sbuf_tensor("p3_emapf", [15, 1], F32))
    iot = E(nc.sbuf_tensor("p3_iot", [128, 128], F32))
    iop = E(nc.sbuf_tensor("p3_iop", [128, 1], F32))
    ones_r = E(nc.sbuf_tensor("p3_ones_r", [1, 128], BF16))
    ones_rf = E(nc.sbuf_tensor("p3_ones_rf", [1, 128], F32))

    P.op("vector", lambda e: e.memset(ones_r[:], 1.0), writes=[("p3.ones",)])
    P.op("vector", lambda e: e.memset(ones_rf[:], 1.0), writes=[("p3.onesf",)])
    P.op("gpsimd", lambda e: e.iota(iot[:], pattern=[[1, 128]], channel_multiplier=0,
                                    allow_small_or_imprecise_dtypes=True),
         writes=[("p3.iot",)])
    P.op("gpsimd", lambda e: e.iota(iop[:], pattern=[[0, 1]], channel_multiplier=1,
                                    allow_small_or_imprecise_dtypes=True),
         writes=[("p3.iop",)])
    P.op("vector", lambda e: e.tensor_scalar(identf[:], iot[:], iop[:, 0:1], None,
                                             ALU.is_equal),
         reads=[("p3.iot",), ("p3.iop",)], writes=[("p3.id",)])
    P.dma("qs", emapi[:], emap.rearrange("(p a) -> p a", a=1), writes=[("p3.em",)])
    P.op("vector", lambda e: e.tensor_copy(emapf[:], emapi[:]),
         reads=[("p3.em",)], writes=[("p3.emf",)])
    P.op("vector", lambda e: e.tensor_scalar(Mb[:], iot[0:15, 0:8], emapf[:, 0:1],
                                             2.0, ALU.is_equal, ALU.mult),
         reads=[("p3.iot",), ("p3.emf",)], writes=[("p3.Mb",)])
    P.op("gpsimd", lambda e: e.iota(Sel8[:], pattern=[[0, 16], [1, 8]],
                                    channel_multiplier=0,
                                    allow_small_or_imprecise_dtypes=True),
         writes=[("p3.sel",)])
    P.op("vector", lambda e: e.tensor_scalar(Sel8[:], Sel8[:], iop[0:8, 0:1], None,
                                             ALU.is_equal),
         reads=[("p3.sel",), ("p3.iop",)], writes=[("p3.sel",)])
    P.dma("qg", rwb[:], rWT.rearrange("p a f -> p (a f)"), writes=[("p3.rw",)])
    P.dma("qg", ab[:], Aall.rearrange("p a f -> p (a f)"), writes=[("p3.ab",)])
    P.dma("qg", ballb[:], Ball, writes=[("p3.ball",)])
    P.dma("qg", bbrow[:], baseb.rearrange("(a f) -> a f", a=1), writes=[("p3.bb",)])
    P.dma("qs", rbrow[:], rb.rearrange("(a f) -> a f", a=1), writes=[("p3.rbr",)])

    with ExitStack() as sB:
        EB = sB.enter_context
        arena = EB(nc.sbuf_tensor("p3b_arena", [128, 32 * 2048], BF16))
        hwres = EB(nc.sbuf_tensor("p3b_hw", [128, 2048], BF16))
        lg = EB(nc.sbuf_tensor("p3b_lg", [128, 15], F32))
        nmx = EB(nc.sbuf_tensor("p3b_nmx", [128, 1], F32))
        pex = EB(nc.sbuf_tensor("p3b_pex", [128, 15], F32))
        ssum = EB(nc.sbuf_tensor("p3b_ssum", [128, 1], F32))
        rs = EB(nc.sbuf_tensor("p3b_rs", [128, 1], F32))
        probs = EB(nc.sbuf_tensor("p3b_probs", [128, 15], F32))
        m8 = EB(nc.sbuf_tensor("p3b_m8", [128, 8], F32))
        wk_ = EB(nc.sbuf_tensor("p3b_wk", [128, 15], F32))
        msk = EB(nc.sbuf_tensor("p3b_msk", [128, 15], F32))
        wsel = EB(nc.sbuf_tensor("p3b_wsel", [128, 15], F32))
        wsT = EB(nc.sbuf_tensor("p3b_wsT", [15, 512], F32))
        w8b = EB(nc.sbuf_tensor("p3b_w8b", [8, 512], F32))
        wrepf = EB(nc.sbuf_tensor("p3b_wrepf", [128, 512], F32))

        for rg in range(4):
            for c in range(32):
                P.dma("qs", arena[:, c * 2048 + rg * 512: c * 2048 + rg * 512 + 512],
                      xn3h[rg * 512:(rg + 1) * 512, c * 128:(c + 1) * 128],
                      transpose=True,
                      reads=[("p3a.xnh", tt) for tt in range(rg * 4, rg * 4 + 4)],
                      writes=[("p3.ar", rg)])

        with ExitStack() as sB2:
            EB2 = sB2.enter_context
            plg = EB2(nc.psum_tensor("p3b_plg", [128, 512], F32))
            pwt = EB2(nc.psum_tensor("p3b_pwt", [128, 512], F32))
            pw8 = EB2(nc.psum_tensor("p3b_pw8", [8, 512], F32))
            pwr = EB2(nc.psum_tensor("p3b_pwr", [128, 512], F32))
            ph = EB2(nc.psum_tensor("p3b_ph", [128, 512], F32))
            prb = EB2(nc.psum_tensor("p3b_prb", [128, 512], F32))
            P.op("tensor", lambda e: e.matmul(prb[:, 0:15], ones_rf[:], rbrow[:],
                                              start=True, stop=True),
                 reads=[("p3.rbr",), ("p3.onesf",)], writes=[("p3.prb",)])
            P.op("vector", lambda e: e.tensor_copy(rbb[:], prb[:, 0:15]),
                 reads=[("p3.prb",)], writes=[("p3.rbb",), ("p3.prb",)])

            for grp in range(4):
                for tt in range(4):
                    t = grp * 4 + tt
                    klg = ("p3b.lg",)
                    for c in range(32):
                        P.op("tensor", lambda e, c=c, t=t: e.matmul(
                            plg[:, 0:15],
                            arena[:, c * 2048 + t * 128: c * 2048 + t * 128 + 128],
                            rwb[:, c * 15:(c + 1) * 15],
                            start=(c == 0), stop=(c == 31)),
                            reads=[("p3.ar", grp), ("p3.rw",)], writes=[klg])
                    P.op("vector", lambda e: e.tensor_add(lg[:], plg[:, 0:15], rbb[:]),
                         reads=[klg, ("p3.rbb",)], writes=[("p3b.l2",), klg])
                    P.op("vector", lambda e: e.reduce_max(nmx[:], lg[:], axis=AX.X,
                                                          negate=True),
                         reads=[("p3b.l2",)], writes=[("p3b.nmx",)])
                    P.op("scalar", lambda e: e.activation(out=pex[:], in_=lg[:],
                                                          func=AF.Exp,
                                                          bias=nmx[:, 0:1],
                                                          accum_out=ssum[:]),
                         reads=[("p3b.l2",), ("p3b.nmx",)], writes=[("p3b.pex",)])
                    P.op("vector", lambda e: e.reciprocal(rs[:], ssum[:]),
                         reads=[("p3b.pex",)], writes=[("p3b.rs",)])
                    P.op("vector", lambda e: e.tensor_scalar(probs[:], pex[:],
                                                             rs[:, 0:1], None,
                                                             ALU.mult),
                         reads=[("p3b.pex",), ("p3b.rs",)], writes=[("p3b.pr",)])
                    P.op("vector", lambda e: e.max(m8[:], probs[:]),
                         reads=[("p3b.pr",)], writes=[("p3b.m8",)])
                    P.op("vector", lambda e: e.match_replace(wk_[:], m8[:], probs[:],
                                                             -1.0),
                         reads=[("p3b.m8",), ("p3b.pr",)], writes=[("p3b.wk",)])
                    P.op("vector", lambda e: e.tensor_single_scalar(msk[:], wk_[:],
                                                                    0.0, ALU.is_lt),
                         reads=[("p3b.wk",)], writes=[("p3b.msk",)])
                    P.op("vector", lambda e: e.tensor_mul(wsel[:], msk[:], probs[:]),
                         reads=[("p3b.msk",), ("p3b.pr",)], writes=[("p3b.ws",)])
                    P.op("tensor", lambda e, tt=tt: e.transpose(
                        pwt[0:15, tt * 128:(tt + 1) * 128], wsel[:], identf[:]),
                        reads=[("p3b.ws",), ("p3.id",)], writes=[("p3b.pwt",)])
                P.op("vector", lambda e: e.tensor_copy(wsT[:], pwt[0:15, 0:512]),
                     reads=[("p3b.pwt",)], writes=[("p3b.wsT",), ("p3b.pwt",)])
                P.op("tensor", lambda e: e.matmul(pw8[:], Mb[:], wsT[:],
                                                  start=True, stop=True),
                     reads=[("p3b.wsT",), ("p3.Mb",)], writes=[("p3b.pw8",)])
                P.op("vector", lambda e: e.tensor_copy(w8b[:], pw8[:]),
                     reads=[("p3b.pw8",)], writes=[("p3b.w8b",), ("p3b.pw8",)])
                P.op("tensor", lambda e: e.matmul(pwr[:], Sel8[:], w8b[:],
                                                  start=True, stop=True),
                     reads=[("p3b.w8b",), ("p3.sel",)], writes=[("p3b.pwr",)])
                P.op("vector", lambda e: e.tensor_copy(wrepf[:], pwr[:]),
                     reads=[("p3b.pwr",)], writes=[("p3b.wrf",), ("p3b.pwr",)])
                kh = ("p3b.ph",)
                for c in range(32):
                    P.op("tensor", lambda e, c=c, grp=grp: e.matmul(
                        ph[:], ab[:, c * 128:(c + 1) * 128],
                        arena[:, c * 2048 + grp * 512: c * 2048 + grp * 512 + 512],
                        start=(c == 0), stop=(c == 31)),
                        reads=[("p3.ab",), ("p3.ar", grp)], writes=[kh])
                P.op("vector", lambda e, grp=grp: e.tensor_mul(
                    hwres[:, grp * 512:(grp + 1) * 512], ph[:], wrepf[:]),
                    reads=[kh, ("p3b.wrf",)], writes=[("p3b.hw",), kh])
        with ExitStack() as sC2:
            EC2 = sC2.enter_context
            pan = [EC2(nc.sbuf_tensor(f"p3c_pan{i}", [128, 16 * 512], BF16))
                   for i in range(2)]
            xr = [EC2(nc.sbuf_tensor(f"p3c_xr{i}", [128, 512], F32)) for i in range(2)]
            ost = [EC2(nc.sbuf_tensor(f"p3c_os{i}", [128, 512], F32)) for i in range(2)]
            pc = [EC2(nc.psum_tensor(f"p3c_p{t}", [128, 512], F32)) for t in range(8)]
            for oc in range(8):
                wbs = (0, 1)
                for chf in range(2):
                    P.dma("qg", pan[wbs[chf]][:],
                          bWT[oc:oc + 1, :, chf * 16:(chf + 1) * 16, :].rearrange(
                              "o p a f -> p (o a f)"),
                          writes=[("p3c.w", wbs[chf])])
                for th in range(2):
                    for t in range(8):
                        P.op("tensor", lambda e, t=t, oc=oc: e.matmul(
                            pc[t][:], ones_r[:], bbrow[:, oc * 512:(oc + 1) * 512],
                            start=True, stop=False),
                            reads=[("p3.bb",), ("p3.ones",)], writes=[("p3c.p", t)])
                    for c in range(32):
                        wb = wbs[c // 16]
                        kw = ("p3c.w", wb)
                        for t in range(8):
                            P.op("tensor", lambda e, c=c, t=t, th=th, wb=wb: e.matmul(
                                pc[t][:],
                                arena[:, c * 2048 + th * 1024 + t * 128:
                                      c * 2048 + th * 1024 + t * 128 + 128],
                                pan[wb][:, (c % 16) * 512:(c % 16) * 512 + 512],
                                start=False, stop=False),
                                reads=[("p3.ar", th * 2 + t // 4), kw],
                                writes=[("p3c.p", t)])
                    for t in range(8):
                        P.op("tensor", lambda e, t=t, th=th, oc=oc: e.matmul(
                            pc[t][:],
                            hwres[:, th * 1024 + t * 128: th * 1024 + t * 128 + 128],
                            ballb[:, oc * 512:(oc + 1) * 512],
                            start=False, stop=True),
                            reads=[("p3b.hw",), ("p3.ball",)], writes=[("p3c.p", t)])
                    for t in range(8):
                        b = t % 2
                        kxr = ("p3c.xr", b)
                        row0 = th * 1024 + t * 128
                        P.dma("qs", xr[b][:],
                              x3[row0:row0 + 128, oc * 512:(oc + 1) * 512],
                              writes=[kxr])
                        ko = ("p3c.o", b)
                        P.op("vector", lambda e, t=t, b=b: e.scalar_tensor_tensor(
                            ost[b][:], pc[t][:], 0.0, xr[b][:], ALU.add, ALU.add),
                            reads=[("p3c.p", t), kxr], writes=[ko, ("p3c.p", t)])
                        P.dma("qs", out_[row0:row0 + 128, oc * 512:(oc + 1) * 512],
                              ost[b][:], reads=[ko], writes=[("outh",)])
    P.barrier()



def _region_rows(t0):
    r0 = (t0 - 256) % S
    return (r0 + np.arange(G)) % S, r0


def _build_p1_only(launch_inputs_shape_check=None):
    nc = bass.Bass("TRN2", target_bir_lowering=False, debug=False)
    nc.reset()
    x1 = nc.dram_tensor("x1", [G, NT], F32, kind="ExternalInput").ap()
    wblk = nc.dram_tensor("wblk", [512, NT], I8, kind="ExternalInput").ap()
    x2 = nc.dram_tensor("x2", [G, NT], F32, kind="ExternalOutput").ap()
    with ExitStack() as ctx:
        block = ctx.enter_context(nc.Block())
        P = Prog(nc)
        build_p1(nc, P, ctx, x1, x2, wblk)
        P.barrier()
        P.resolve()
        P.emit(block)
        P._sem_stack.close()
    return nc


def _build_p0_only():
    nc = bass.Bass("TRN2", target_bir_lowering=False, debug=False)
    nc.reset()
    io = {}
    io["xbat"] = nc.dram_tensor("xbat", [S, NT], F32, kind="ExternalInput").ap()
    io["xreg"] = nc.dram_tensor("xreg", [G, NT], F32, kind="ExternalInput").ap()
    io["wqkv"] = nc.dram_tensor("wqkv", [24, 128, 32, 128], F32, kind="ExternalInput").ap()
    io["bqkv"] = nc.dram_tensor("bqkv", [3072], F32, kind="ExternalInput").ap()
    io["wo0"] = nc.dram_tensor("wo0", [1024, NT], F32, kind="ExternalInput").ap()
    io["bo0"] = nc.dram_tensor("bo0", [NT], F32, kind="ExternalInput").ap()
    io["x1"] = nc.dram_tensor("x1", [G, NT], F32, kind="ExternalOutput").ap()
    io["xn"] = nc.dram_tensor("xn_s", [S, NT], BF16, kind="Internal").ap()
    io["xnreg"] = nc.dram_tensor("xnreg_s", [G, NT], BF16, kind="Internal").ap()
    io["kT"] = nc.dram_tensor("kT_s", [1024, S], BF16, kind="Internal").ap()
    io["vT"] = nc.dram_tensor("vT_s", [1024, S], BF16, kind="Internal").ap()
    io["qT"] = nc.dram_tensor("qT_s", [1024, G], BF16, kind="Internal").ap()
    io["oT"] = nc.dram_tensor("oT_s", [1024, G], BF16, kind="Internal").ap()
    io["wkvst"] = nc.dram_tensor("wkvst_s", [2048, NT], BF16, kind="Internal").ap()
    with ExitStack() as ctx:
        block = ctx.enter_context(nc.Block())
        P = Prog(nc)
        build_p0(nc, P, ctx, io)
        P.resolve()
        P.emit(block)
        P._sem_stack.close()
    return nc


_NC_CACHE = {}


def _get_nc(name, builder):
    if name not in _NC_CACHE:
        _NC_CACHE[name] = builder()
    return _NC_CACHE[name]


def _build_fused():
    nc = bass.Bass("TRN2", target_bir_lowering=False, debug=False)
    nc.reset()
    io = {}
    io["xbat"] = nc.dram_tensor("xbat", [S, NT], F32, kind="ExternalInput").ap()
    io["xreg"] = nc.dram_tensor("xreg", [G, NT], F32, kind="ExternalInput").ap()
    io["wqkv"] = nc.dram_tensor("wqkv", [24, 128, 32, 128], F32, kind="ExternalInput").ap()
    io["bqkv"] = nc.dram_tensor("bqkv", [3072], F32, kind="ExternalInput").ap()
    io["wo0"] = nc.dram_tensor("wo0", [1024, NT], F32, kind="ExternalInput").ap()
    io["bo0"] = nc.dram_tensor("bo0", [NT], F32, kind="ExternalInput").ap()
    io["wblk"] = nc.dram_tensor("wblk", [512, NT], I8, kind="ExternalInput").ap()
    io["wqT"] = nc.dram_tensor("wqT", [32, 128, 32, 128], F32, kind="ExternalInput").ap()
    io["wkT"] = nc.dram_tensor("wkT", [128, 32, 128], F32, kind="ExternalInput").ap()
    io["wvT"] = nc.dram_tensor("wvT", [128, 32, 128], F32, kind="ExternalInput").ap()
    io["wo1T"] = nc.dram_tensor("wo1T", [8, 128, 32, 512], F32, kind="ExternalInput").ap()
    io["pos"] = nc.dram_tensor("pos", [1, G], F32, kind="ExternalInput").ap()
    io["kmask"] = nc.dram_tensor("kmask", [128, 4], F32, kind="ExternalInput").ap()
    io["amask"] = nc.dram_tensor("amask", [128, 1], F32, kind="ExternalInput").ap()
    io["invfreq"] = nc.dram_tensor("invfreq", [128, 1], F32, kind="ExternalInput").ap()
    io["bWT"] = nc.dram_tensor("bWT", [8, 128, 32, 512], F32, kind="ExternalInput").ap()
    io["baseb"] = nc.dram_tensor("baseb", [NT], F32, kind="ExternalInput").ap()
    io["rWT"] = nc.dram_tensor("rWT", [128, 32, 15], F32, kind="ExternalInput").ap()
    io["rb"] = nc.dram_tensor("rb", [15], F32, kind="ExternalInput").ap()
    io["Aall"] = nc.dram_tensor("Aall", [128, 32, 128], F32, kind="ExternalInput").ap()
    io["Ball"] = nc.dram_tensor("Ball", [128, NT], F32, kind="ExternalInput").ap()
    io["emap"] = nc.dram_tensor("emap", [15], I32, kind="ExternalInput").ap()
    io["out"] = nc.dram_tensor("out", [OWN, NT], F32, kind="ExternalOutput").ap()
    io["x1"] = nc.dram_tensor("x1_s", [G, NT], F32, kind="Internal").ap()
    io["x2"] = nc.dram_tensor("x2_s", [G, NT], F32, kind="Internal").ap()
    io["x3"] = nc.dram_tensor("x3_s", [OWN, NT], F32, kind="Internal").ap()
    io["xn"] = nc.dram_tensor("xn_s", [S, NT], BF16, kind="Internal").ap()
    io["xnreg"] = nc.dram_tensor("xnreg_s", [G, NT], BF16, kind="Internal").ap()
    io["kT"] = nc.dram_tensor("kT_s", [1024, S], BF16, kind="Internal").ap()
    io["vT"] = nc.dram_tensor("vT_s", [1024, S], BF16, kind="Internal").ap()
    io["qT"] = nc.dram_tensor("qT_s", [1024, G], BF16, kind="Internal").ap()
    io["oT"] = nc.dram_tensor("oT_s", [1024, G], BF16, kind="Internal").ap()
    io["xn2"] = nc.dram_tensor("xn2_s", [G, NT], BF16, kind="Internal").ap()
    io["qT2"] = nc.dram_tensor("qT2_s", [NT, G], BF16, kind="Internal").ap()
    io["oT2"] = nc.dram_tensor("oT2_s", [NT, OWN], BF16, kind="Internal").ap()
    io["xn3"] = nc.dram_tensor("xn3_s", [OWN, NT], BF16, kind="Internal").ap()
    with ExitStack() as ctx:
        block = ctx.enter_context(nc.Block())
        P = Prog(nc)
        with ExitStack() as c0:
            build_p0(nc, P, c0, io)
        with ExitStack() as c1:
            build_p1(nc, P, c1, io["x1"], io["x2"], io["xn2"], io["wblk"])
        P.barrier()
        with ExitStack() as c2:
            build_p2(nc, P, c2, io)
        with ExitStack() as c3:
            build_p3(nc, P, c3, io)
        P.resolve()
        P.emit(block)
        P._sem_stack.close()
    return nc


def kernel(L0_attn_x, L0_attn_Wqkv, L0_attn_bqkv, L0_attn_Wo, L0_attn_bo,
           L0_ffn_kernel, L1_attn_Wq, L1_attn_Wk, L1_attn_Wv, L1_attn_Wo,
           L1_ffn_base_W, L1_ffn_base_b, L1_ffn_router_W, L1_ffn_router_b,
           L1_ffn_A, L1_ffn_B, L1_ffn_expert_map):
    x = np.asarray(L0_attn_x)
    assert x.shape == (2, S, NT)
    ncores = 8

    wqkv = np.ascontiguousarray(
        np.asarray(L0_attn_Wqkv).reshape(32, 128, 24, 128).transpose(2, 1, 0, 3))
    bqkv = np.ascontiguousarray(L0_attn_bqkv)
    wo0 = np.ascontiguousarray(L0_attn_Wo)
    bo0 = np.ascontiguousarray(L0_attn_bo)
    wblk = np.ascontiguousarray(np.asarray(L0_ffn_kernel).reshape(512, NT))
    wqT = np.ascontiguousarray(
        np.asarray(L1_attn_Wq).T.reshape(32, 128, 32, 128).transpose(2, 1, 0, 3))
    wkT = np.ascontiguousarray(
        np.asarray(L1_attn_Wk).T.reshape(32, 128, 128).transpose(1, 0, 2))
    wvT = np.ascontiguousarray(
        np.asarray(L1_attn_Wv).T.reshape(32, 128, 128).transpose(1, 0, 2))
    wo1T = np.ascontiguousarray(
        np.asarray(L1_attn_Wo).T.reshape(32, 128, 8, 512).transpose(2, 1, 0, 3))
    bWT = np.ascontiguousarray(
        np.asarray(L1_ffn_base_W).T.reshape(32, 128, 8, 512).transpose(2, 1, 0, 3))
    baseb = np.ascontiguousarray(L1_ffn_base_b)
    rWT = np.ascontiguousarray(
        np.asarray(L1_ffn_router_W).T.reshape(32, 128, 15).transpose(1, 0, 2))
    rb = np.ascontiguousarray(L1_ffn_router_b)
    Aall = np.ascontiguousarray(
        np.asarray(L1_ffn_A).transpose(1, 0, 2).reshape(128, NT).T
        .reshape(32, 128, 128).transpose(1, 0, 2))
    Ball = np.ascontiguousarray(
        np.asarray(L1_ffn_B).transpose(2, 0, 1).reshape(128, NT))
    emap = np.ascontiguousarray(np.asarray(L1_ffn_expert_map).astype(np.int32))
    invfreq = (1.0 / 10000.0 ** (2.0 * (np.arange(128) // 2) / 128.0)
               ).astype(np.float32).reshape(128, 1)

    in_maps = []
    meta = []
    for c in range(ncores):
        b, t0 = c // 4, (c % 4) * 2048
        rows = (np.arange(G) + (t0 - 256)) % S
        km = np.zeros((128, 4), np.float32)
        for ks in range(4):
            absk = t0 - 256 + 2048 + ks * 128 + np.arange(128)
            km[:, ks] = (absk < S).astype(np.float32)
        in_maps.append(dict(
            xbat=np.ascontiguousarray(x[b]),
            xreg=np.ascontiguousarray(x[b][rows]),
            wqkv=wqkv, bqkv=bqkv, wo0=wo0, bo0=bo0, wblk=wblk,
            wqT=wqT, wkT=wkT, wvT=wvT, wo1T=wo1T,
            pos=rows.astype(np.float32).reshape(1, G), kmask=km,
            amask=np.full((128, 1), 1.0 if t0 == 0 else 0.0, np.float32),
            invfreq=invfreq, bWT=bWT, baseb=baseb, rWT=rWT, rb=rb,
            Aall=Aall, Ball=Ball, emap=emap))
        meta.append((b, t0))

    nc = _get_nc("fused", _build_fused)
    res = run_bass_kernel_spmd(nc, in_maps, core_ids=list(range(ncores)))
    out = np.empty((2, S, NT), np.float32)
    for c in range(ncores):
        b, t0 = meta[c]
        out[b, t0:t0 + OWN] = res.results[c]["out"]
    return out


def _build_p2_only():
    nc = bass.Bass("TRN2", target_bir_lowering=False, debug=False)
    nc.reset()
    io = {}
    io["x2"] = nc.dram_tensor("x2", [G, NT], F32, kind="ExternalInput").ap()
    io["wqT"] = nc.dram_tensor("wqT", [32, 128, 32, 128], F32, kind="ExternalInput").ap()
    io["wkT"] = nc.dram_tensor("wkT", [128, 32, 128], F32, kind="ExternalInput").ap()
    io["wvT"] = nc.dram_tensor("wvT", [128, 32, 128], F32, kind="ExternalInput").ap()
    io["wo1T"] = nc.dram_tensor("wo1T", [8, 128, 32, 512], F32, kind="ExternalInput").ap()
    io["pos"] = nc.dram_tensor("pos", [1, G], F32, kind="ExternalInput").ap()
    io["kmask"] = nc.dram_tensor("kmask", [128, 4], F32, kind="ExternalInput").ap()
    io["amask"] = nc.dram_tensor("amask", [128, 1], F32, kind="ExternalInput").ap()
    io["invfreq"] = nc.dram_tensor("invfreq", [128, 1], F32, kind="ExternalInput").ap()
    io["x3"] = nc.dram_tensor("x3", [OWN, NT], F32, kind="ExternalOutput").ap()
    io["xn2"] = nc.dram_tensor("xn2_s", [G, NT], BF16, kind="Internal").ap()
    io["qT2"] = nc.dram_tensor("qT2_s", [NT, G], BF16, kind="Internal").ap()
    io["oT2"] = nc.dram_tensor("oT2_s", [NT, OWN], BF16, kind="Internal").ap()
    with ExitStack() as ctx:
        block = ctx.enter_context(nc.Block())
        P = Prog(nc)
        build_p2(nc, P, ctx, io)
        P.resolve()
        P.emit(block)
        P._sem_stack.close()
    return nc


def _build_p3_only():
    nc = bass.Bass("TRN2", target_bir_lowering=False, debug=False)
    nc.reset()
    io = {}
    io["x3"] = nc.dram_tensor("x3", [OWN, NT], F32, kind="ExternalInput").ap()
    io["bWT"] = nc.dram_tensor("bWT", [8, 128, 32, 512], F32, kind="ExternalInput").ap()
    io["baseb"] = nc.dram_tensor("baseb", [NT], F32, kind="ExternalInput").ap()
    io["rWT"] = nc.dram_tensor("rWT", [128, 32, 15], F32, kind="ExternalInput").ap()
    io["rb"] = nc.dram_tensor("rb", [15], F32, kind="ExternalInput").ap()
    io["Aall"] = nc.dram_tensor("Aall", [128, 32, 128], F32, kind="ExternalInput").ap()
    io["Ball"] = nc.dram_tensor("Ball", [128, NT], F32, kind="ExternalInput").ap()
    io["emap"] = nc.dram_tensor("emap", [15], I32, kind="ExternalInput").ap()
    io["out"] = nc.dram_tensor("out", [OWN, NT], F32, kind="ExternalOutput").ap()
    io["xn3"] = nc.dram_tensor("xn3_s", [OWN, NT], BF16, kind="Internal").ap()
    with ExitStack() as ctx:
        block = ctx.enter_context(nc.Block())
        P = Prog(nc)
        build_p3(nc, P, ctx, io)
        P.resolve()
        P.emit(block)
        P._sem_stack.close()
    return nc
